# version 25
# baseline (speedup 1.0000x reference)
import numpy as np
import ml_dtypes
import concourse.bass as bass
import concourse.mybir as mybir
from concourse.bass_utils import run_bass_kernel_spmd

F32 = mybir.dt.float32
BF16 = mybir.dt.bfloat16
U8 = mybir.dt.uint8
AF = mybir.ActivationFunctionType
ALU = mybir.AluOpType
AX = mybir.AxisListType.X

D = 1024
QKV = 3072
H = 4096
NH, DH, GH = 16, 64, 192
EPS = 1e-5

C_EPS, C_NINV, C_INV, C_NHALF, C_NEG1 = 0, 1, 2, 3, 4
SUM1, SUMSQ1, NEGM1, TMP1, VAR1, LNV1, RSTD1, BB1 = 0, 1, 2, 3, 4, 5, 6, 7
RSTD1B = 17
WSUM, WSUMSQ, WNEGM, WTMP, WBB, WLNV, WRSTD = 19, 20, 21, 22, 23, 24, 25
SUM2, SUMSQ2, NEGM2, TMP2, VAR2, LNV2, RSTD2, NMR2, BB2 = 8, 9, 10, 11, 12, 13, 14, 15, 16
A_S, A_M, A_SM, A_NM, A_U, A_TT, A_Z, A_R, A_A, A_B = range(10)


class Sched:

    ENGS = ("pe", "ve", "ac", "sp", "pl")

    SEMS = ("pe", "ve", "ac", "po", "pc", "pwq", "pw1", "pw2", "pbld",
            "sx0", "sx1", "sx2", "sst0", "sst1", "sb0", "sb1", "sb2", "so0", "so1", "pbld2",
            "ssw0", "ssw1", "ssw2", "ssw3",
            "wq0", "wq1", "wq2", "wq3", "wq4", "wq5")

    def __init__(self):
        self.ops = {e: [] for e in self.ENGS}
        self.cnt = {s: 0 for s in self.SEMS}

    def emit(self, eng, build, waits=(), inc=None, by=1):
        waits = tuple((s, v) for (s, v) in waits if v and v > 0)
        after = None
        if inc is not None:
            self.cnt[inc] += by
            after = self.cnt[inc]
        self.ops[eng].append((build, waits, inc, by))
        return after

    def replay(self, eng, handle, sems):
        seen = {}
        for build, waits, inc, by in self.ops[eng]:
            for s, v in waits:
                if seen.get(s, -1) < v:
                    handle.wait_ge(sems[s], v)
                    seen[s] = v
            ins = build(handle)
            if inc is not None:
                ins.then_inc(sems[inc], by)


def build_nc(S, phase_a_only=False, b_stage=99):
    NCH = S // 128
    NSC = S // 512
    nc = bass.Bass("TRN2", target_bir_lowering=False, debug=False)

    x_d = nc.dram_tensor("x", [S, D], F32, kind="ExternalInput").ap()
    wqkv_d = nc.dram_tensor("wqkv", [D, QKV], F32, kind="ExternalInput").ap()
    w1t_d = nc.dram_tensor("w1t", [D, H], F32, kind="ExternalInput").ap()
    w2t_d = nc.dram_tensor("w2t", [H, D], F32, kind="ExternalInput").ap()
    b1c_d = nc.dram_tensor("b1c", [128, H // 128], F32, kind="ExternalInput").ap()
    b2r_d = nc.dram_tensor("b2r", [1, D], F32, kind="ExternalInput").ap()
    cst_d = nc.dram_tensor("cst", [128, 8], F32, kind="ExternalInput").ap()
    idx_d = nc.dram_tensor("idx", [128, NCH], F32, kind="ExternalInput").ap()
    scal_d = nc.dram_tensor("scal", [128, 1], F32, kind="ExternalInput").ap()
    ubf_d = nc.dram_tensor("ubf", [128, 128], BF16, kind="ExternalInput").ap()
    i128f_d = nc.dram_tensor("i128f", [128, 128], F32, kind="ExternalInput").ap()
    i128b_d = nc.dram_tensor("i128b", [128, 128], BF16, kind="ExternalInput").ap()
    onesr_d = nc.dram_tensor("onesr", [1, 128], BF16, kind="ExternalInput").ap()
    onesc_d = nc.dram_tensor("onesc", [128, 1], BF16, kind="ExternalInput").ap()
    x1s_d = nc.dram_tensor("x1s", [S, D], F32, kind="Internal").ap()
    out_d = nc.dram_tensor("out", [S, D], F32, kind="ExternalOutput").ap()

    wq_union = nc.alloc_sbuf_tensor("wq_union", [128, 65536], U8)
    ubase = nc.lookup_mloc(wq_union).addr
    wqkv_sb = nc.alloc_sbuf_tensor_at("wqkv_sb", [128, 8, QKV], BF16, offset=ubase)
    w2t_sb = nc.alloc_sbuf_tensor_at("w2t_sb", [128, 32, D], BF16, offset=ubase)
    qkv_tm = [nc.alloc_sbuf_tensor_at(f"qkv_tm{i}", [128, QKV], BF16,
                                      offset=ubase + 49152 + i * 6144) for i in range(2)]
    v_bf = [nc.alloc_sbuf_tensor_at(f"vbf{i}", [128, D], BF16,
                                    offset=ubase + 61440 + i * 2048) for i in range(2)]
    w1t_sb = nc.alloc_sbuf_tensor("w1t_sb", [128, 8, H], BF16)
    b1c_sb = nc.alloc_sbuf_tensor("b1c_sb", [128, 32], F32)
    cst = nc.alloc_sbuf_tensor("cst_sb", [128, 8], F32)
    idx_sb = nc.alloc_sbuf_tensor("idx_sb", [128, NCH], F32)
    scal_sb = nc.alloc_sbuf_tensor("scal_sb", [128, 1], F32)
    ubf = nc.alloc_sbuf_tensor("ubf_sb", [128, 128], BF16)
    i128b = nc.alloc_sbuf_tensor("i128b_sb", [128, 128], BF16)
    onesr = nc.alloc_sbuf_tensor("onesr_sb", [1, 128], BF16)
    onesc = nc.alloc_sbuf_tensor("onesc_sb", [128, 1], BF16)
    stat = nc.alloc_sbuf_tensor("stat", [128, 28], F32)
    atile = nc.alloc_sbuf_tensor("atile", [128, 16 * 10], F32)
    junkv = nc.alloc_sbuf_tensor("junkv", [128, 1024], F32)
    junka = nc.alloc_sbuf_tensor("junka", [128, 1024], F32)

    arena = nc.alloc_sbuf_tensor("arena", [128, 70 * 1024], U8)
    abase = nc.lookup_mloc(arena).addr
    off = [0]

    def place(name, shape, dtype):
        per = int(np.prod(shape[1:])) * mybir.dt.size(dtype)
        t = nc.alloc_sbuf_tensor_at(name, shape, dtype, offset=abase + off[0])
        off[0] += (per + 31) & ~31
        return t

    offA_start = off[0]
    x_t = [place(f"x_t{i}", [128, D], F32) for i in range(3)]
    x1o = [place(f"x1o{i}", [128, D], F32) for i in range(2)]
    xc_bf = [place(f"xc{i}", [128, D], BF16) for i in range(2)]
    ln1fm = place("ln1fm", [128, 8, 128], BF16)
    p_sb = place("p_sb", [128, D], F32)
    attn_s = [place(f"attn_s{i}", [128, D], F32) for i in range(2)]
    t2_s = [place(f"t2_s{i}", [128, D], F32) for i in range(2)]
    carry_b = place("carry_b", [1, D], BF16)
    sizeA = off[0] - offA_start

    off[0] = offA_start
    h_fm = place("h_fm", [128, 32, 512], BF16)
    x1i = [place("x1i0", [128, D], F32)]
    b2_bf = place("b2_bf", [1, D], BF16)
    evodfm = place("evodfm", [128, 8, 16], BF16)
    rs_sb = place("rs_sb", [16, D], BF16)
    odd4ext = place("odd4ext", [128, 32, 16], BF16)
    ts4 = place("ts4", [4, 512], BF16)
    gt_bf = place("gt_bf", [16, 512], BF16)
    x1i.append(place("x1i1", [128, D], F32))
    assert off[0] - offA_start <= sizeA + 31, (off[0] - offA_start, sizeA)
    off[0] = offA_start + ((sizeA + 31) & ~31)
    ln3fm = place("ln3fm", [128, 8, 512], BF16)
    x1i.append(place("x1i_w1", [128, D], F32))
    xc3_bf = place("xc3", [128, D], BF16)
    r1_sb = place("r1_sb", [4, H], BF16)
    stag = nc.alloc_sbuf_tensor_at("stag", [1, 1024], BF16,
                                   offset=nc.lookup_mloc(x1o[0]).addr)
    stag2 = nc.alloc_sbuf_tensor_at("stag2", [1, 1024], BF16,
                                    offset=nc.lookup_mloc(h_fm).addr)
    stag4 = nc.alloc_sbuf_tensor_at("stag4", [1, 1024], BF16,
                                    offset=nc.lookup_mloc(p_sb).addr)
    sizeB = off[0] - offA_start
    assert max(sizeA, sizeB) <= 70 * 1024, (sizeA, sizeB)

    pA = nc.alloc_psum_tensor("pA", [128, 1024], F32)
    pB = nc.alloc_psum_tensor("pB", [128, 1024], F32)
    pC = nc.alloc_psum_tensor("pC", [128, 1024], F32)
    pD = nc.alloc_psum_tensor("pD", [128, 1024], F32)

    sch = Sched()
    E = sch.emit
    ja_last = [0]
    jv_last = [0]

    for dst_t, src_t in ((cst, cst_d), (idx_sb, idx_d), (ubf, ubf_d),
                     (i128b, i128b_d), (onesr, onesr_d), (onesc, onesc_d),
                     (scal_sb, scal_d), (b1c_sb, b1c_d)):
        E("pl", (lambda d=dst_t, s=src_t: (lambda h: h.dma_start(d[:, :], s)))(),
          inc="pc", by=16)
    consts_done = sch.cnt["pc"]
    for g in range(6):
        E("pl", (lambda g=g: (lambda h: h.dma_start(
            wqkv_sb[:, :, g * 512:(g + 1) * 512],
            wqkv_d[:, g * 512:(g + 1) * 512].rearrange(
                "(kc p) n -> p kc n", p=128))))(), inc=f"wq{g}", by=16)
    for kc in range(8):
        E("pl", (lambda kc=kc: (lambda h: h.dma_start(
            w1t_sb[:, kc, :], w1t_d[kc * 128:(kc + 1) * 128, :])))(), inc="pw1", by=16)
    w1_done = sch.cnt["pw1"]

    E("ve", lambda h: h.memset(carry_b[:, :], 0.0), inc="ve")
    carry_ready = sch.cnt["ve"]

    ld_x = [0] * NCH
    stx1 = [0] * NCH
    lnfm_pe = [0] * NCH
    lnfmev_ac = [0] * NCH
    qkv_pe = [[0] * 6 for _ in range(NCH)]
    qev = [[0] * 6 for _ in range(NCH)]
    qev_sem = [[None] * 6 for _ in range(NCH)]
    vbf_ve = [0] * NCH
    s_pe = [0] * NCH
    cs_pe = [0] * NCH
    carry_ve = [0] * (NCH + 1)
    carry_ve[0] = carry_ready
    attn_ve = [0] * NCH
    apply_ac = [0] * NCH
    xc_ac = [0] * NCH
    rstd1_ac = [0] * NCH
    sq1_ve = [0] * NCH
    negm1_ac = [0] * NCH
    x1_po = [0] * NCH
    t3_po = [0] * NCH
    t1_ve = [0] * NCH
    t2_ve = [0] * NCH
    ess_ve = [0] * NCH
    ep_po = [0] * NCH
    lnv1_ac = [0] * NCH
    lnv2_ac = [0] * NCH

    def ld_a(c):
        xt = x_t[c % 3]
        rows = slice(c * 128, (c + 1) * 128)
        w = []
        if c >= 3:
            w = [("po", x1_po[c - 3]), ("ac", xc_ac[c - 3]), ("ve", sq1_ve[c - 3])]
        ld_x[c] = E("sp", (lambda xt=xt, rows=rows: (lambda h: h.dma_start(
            xt[:, :], x_d[rows, :])))(), waits=w, inc=f"sx{c % 3}", by=16)

    def head_fast_a(c):
        xt = x_t[c % 3]
        sum1 = E("ac", (lambda xt=xt: (lambda h: h.activation(
            junka[:, :], xt[:, :], AF.Copy, accum_out=stat[:, SUM1:SUM1 + 1])))(),
            waits=[(f"sx{c % 3}", ld_x[c]), ("pc", consts_done), ("ac", ja_last[0])],
            inc="ac")
        ja_last[0] = sum1
        negm1_ac[c] = E("ac", lambda h: h.activation(
            stat[:, NEGM1:NEGM1 + 1], stat[:, SUM1:SUM1 + 1], AF.Copy,
            scale=cst[:, C_NINV:C_NINV + 1]), waits=[("ac", sum1)], inc="ac")
        w_xc = [("ac", negm1_ac[c])]
        if c >= 2:
            w_xc.append(("pe", lnfm_pe[c - 2]))
        xc_ac[c] = E("ac", (lambda xt=xt, xcb=xc_bf[c % 2]: (lambda h: h.activation(
            xcb[:, :], xt[:, :], AF.Identity, bias=stat[:, NEGM1:NEGM1 + 1])))(),
            waits=w_xc, inc="ac")

    def head_slow_a(c):
        xt = x_t[c % 3]
        rcol = RSTD1 if c % 2 == 0 else RSTD1B
        w_sq = [(f"sx{c % 3}", ld_x[c]), ("ve", jv_last[0])]
        if c >= 1:
            w_sq.append(("ac", lnv1_ac[c - 1]))
        sq1 = E("ve", (lambda xt=xt: (lambda h: h.scalar_tensor_tensor(
            junkv[:, :], xt[:, :], 1.0, xt[:, :], ALU.mult, ALU.mult,
            accum_out=stat[:, SUMSQ1:SUMSQ1 + 1])))(), waits=w_sq, inc="ve")
        jv_last[0] = sq1
        sq1_ve[c] = sq1
        p_ = E("ac", lambda h: h.activation(
            stat[:, TMP1:TMP1 + 1], stat[:, NEGM1:NEGM1 + 1], AF.Square),
            waits=[("ac", negm1_ac[c])], inc="ac")
        p_ = E("ac", lambda h: h.activation(
            stat[:, BB1:BB1 + 1], stat[:, TMP1:TMP1 + 1], AF.Identity,
            scale=cst[:, C_NEG1:C_NEG1 + 1], bias=cst[:, C_EPS:C_EPS + 1]),
            waits=[("ac", p_)], inc="ac")
        w_lnv = [("ve", sq1), ("ac", p_)]
        if c >= 2:
            w_lnv.append(("ve", qev[c - 2][4]))
        lnv1_ac[c] = p_ = E("ac", lambda h: h.activation(
            stat[:, LNV1:LNV1 + 1], stat[:, SUMSQ1:SUMSQ1 + 1], AF.Ln,
            scale=cst[:, C_INV:C_INV + 1], bias=stat[:, BB1:BB1 + 1]),
            waits=w_lnv, inc="ac")
        rstd1_ac[c] = E("ac", (lambda rcol=rcol: (lambda h: h.activation(
            stat[:, rcol:rcol + 1], stat[:, LNV1:LNV1 + 1], AF.Exp,
            scale=cst[:, C_NHALF:C_NHALF + 1])))(), waits=[("ac", p_)], inc="ac")

    def pe1_a(c):
        xcb = xc_bf[c % 2]
        for kc in range(8):
            w = []
            if kc == 0:
                w = [("ac", xc_ac[c])] + list(pa_war[0])
            E("pe", (lambda kc=kc, xcb=xcb: (lambda h: h.matmul(
                pA[:, kc * 128:(kc + 1) * 128], xcb[:, kc * 128:(kc + 1) * 128],
                i128b[:, :], start=True, stop=True)))(), waits=w,
                inc="pe" if kc == 7 else None)
        lnfm_pe[c] = sch.cnt["pe"]
        w = [("pe", lnfm_pe[c])]
        if c >= 1:
            w.append(("pe", qkv_pe[c - 1][5]))
        lnfmev_ac[c] = E("ve", lambda h: h.tensor_copy(
            ln1fm[:, :, :], pA[:, :].rearrange("p (a b) -> p a b", a=8)),
            waits=w, inc="ve")
        pa_war[0] = [("ve", lnfmev_ac[c])]

    pa_war = [[]]
    ldx1 = [[0] * 4 for _ in range(NSC)]
    ldb = [[0] * 4 for _ in range(NSC)]
    l3ev = [[0] * 4 for _ in range(NSC)]
    xc3_ac = [[0] * 4 for _ in range(NSC)]
    sq3_ve = [[0] * 4 for _ in range(NSC)]
    ln3_pe = [[0] * 4 for _ in range(NSC)]
    ts_pe = [0] * NSC
    ts4_ve = [0] * NSC
    gel_ac = [[0] * 32 for _ in range(NSC)]
    gt_pe = [0] * NSC
    gtbf_ve = [0] * NSC
    o2_pe = [[0] * 4 for _ in range(NSC)]
    fin_ve = [[0] * 4 for _ in range(NSC)]
    sum4_ac = [[0] * 4 for _ in range(NSC)]
    stout = [[0] * 4 for _ in range(NSC)]
    nmr4_ac = [[0] * 4 for _ in range(NSC)]
    h_pe = [[0] * 32 for _ in range(NSC)]
    lnv3_ac = [[0] * 4 for _ in range(NSC)]
    lnv4_ac = [[0] * 4 for _ in range(NSC)]
    build = {}

    def w1_load_stats(sc, s):
        cglob = sc * 4 + s
        slot = x1i[2]
        w = []
        if sc == 0:
            w.append((f"ssw{s}", 16))
            if s > 0:
                w.append(("ac", xc3_ac[0][s - 1]))
                w.append(("ve", sq3_ve[0][s - 1]))
        else:
            psc, ps = (sc, s - 1) if s > 0 else (sc - 1, 3)
            w.append(("ac", xc3_ac[psc][ps]))
            w.append(("ve", sq3_ve[psc][ps]))
        ldx1[sc][s] = E("sp", (lambda slot=slot, cglob=cglob: (lambda h: h.dma_start(
            slot[:, :], x1s_d[cglob * 128:(cglob + 1) * 128, :])))(),
            waits=w, inc="sb2", by=16)
        w_sq = [("sb2", ldx1[sc][s]), ("ve", jv_last[0])]
        prev_lnv3 = (lnv3_ac[sc][s - 1] if s > 0 else
                     (lnv3_ac[sc - 1][3] if sc > 0 else 0))
        if prev_lnv3:
            w_sq.append(("ac", prev_lnv3))
        sq3 = E("ve", (lambda slot=slot: (lambda h: h.scalar_tensor_tensor(
            junkv[:, :], slot[:, :], 1.0, slot[:, :], ALU.mult, ALU.mult,
            accum_out=stat[:, WSUMSQ:WSUMSQ + 1])))(), waits=w_sq, inc="ve")
        jv_last[0] = sq3
        sq3_ve[sc][s] = sq3
        w_s3 = [("sb2", ldx1[sc][s]), ("ac", ja_last[0])]
        sum3 = E("ac", (lambda slot=slot: (lambda h: h.activation(
            junka[:, :], slot[:, :], AF.Copy, accum_out=stat[:, WSUM:WSUM + 1])))(),
            waits=w_s3, inc="ac")
        ja_last[0] = sum3
        p_ = E("ac", lambda h: h.activation(
            stat[:, WNEGM:WNEGM + 1], stat[:, WSUM:WSUM + 1], AF.Copy,
            scale=cst[:, C_NINV:C_NINV + 1]), waits=[("ac", sum3)], inc="ac")
        p_ = E("ac", lambda h: h.activation(
            stat[:, WTMP:WTMP + 1], stat[:, WNEGM:WNEGM + 1], AF.Square),
            waits=[("ac", p_)], inc="ac")
        p_ = E("ac", lambda h: h.activation(
            stat[:, WBB:WBB + 1], stat[:, WTMP:WTMP + 1], AF.Identity,
            scale=cst[:, C_NEG1:C_NEG1 + 1], bias=cst[:, C_EPS:C_EPS + 1]),
            waits=[("ac", p_)], inc="ac")
        lnv3_ac[sc][s] = p_ = E("ac", lambda h: h.activation(
            stat[:, WLNV:WLNV + 1], stat[:, WSUMSQ:WSUMSQ + 1], AF.Ln,
            scale=cst[:, C_INV:C_INV + 1], bias=stat[:, WBB:WBB + 1]),
            waits=[("ve", sq3), ("ac", p_)], inc="ac")
        p_ = E("ac", lambda h: h.activation(
            stat[:, WRSTD:WRSTD + 1], stat[:, WLNV:WLNV + 1], AF.Exp,
            scale=cst[:, C_NHALF:C_NHALF + 1]), waits=[("ac", p_)], inc="ac")
        p_ = E("ac", lambda h: h.activation(
            stat[:, WBB:WBB + 1], stat[:, WNEGM:WNEGM + 1], AF.Copy,
            scale=stat[:, WRSTD:WRSTD + 1]), waits=[("ac", p_)], inc="ac")
        w_xc = [("ac", p_)]
        psc, ps = (sc, s - 1) if s > 0 else ((sc - 1, 3) if sc > 0 else (None, None))
        if psc is not None:
            w_xc.append(("pe", ln3_pe[psc][ps]))
        xc3_ac[sc][s] = E("ac", (lambda slot=slot: (lambda h: h.activation(
            xc3_bf[:, :], slot[:, :], AF.Identity, bias=stat[:, WBB:WBB + 1],
            scale=stat[:, WRSTD:WRSTD + 1])))(), waits=w_xc, inc="ac")

    def w1_mms(sc, s):
        for kc in range(8):
            w = []
            if kc == 0:
                w = [("ac", xc3_ac[sc][s])] + list(pa_war[0])
            E("pe", (lambda kc=kc: (lambda h: h.matmul(
                pA[:, kc * 128:(kc + 1) * 128], xc3_bf[:, kc * 128:(kc + 1) * 128],
                i128b[:, :], start=True, stop=True)))(), waits=w,
                inc="pe" if kc == 7 else None)
        ln3_pe[sc][s] = sch.cnt["pe"]
        w = [("pe", ln3_pe[sc][s])]
        if sc >= 1 and s == 0:
            w.append(("pe", h_pe[sc - 1][31]))
        l3ev[sc][s] = E("ac", (lambda s=s: (lambda h: h.copy(
            ln3fm[:, :, s * 128:(s + 1) * 128],
            pA[:, :].rearrange("p (a b) -> p a b", a=8))))(), waits=w, inc="ac")
        pa_war[0] = [("ac", l3ev[sc][s])]

    def tst(sc):
        for s2 in range(4):
            for kc in range(8):
                w = []
                if kc == 0:
                    w = [("ac", l3ev[sc][s2])]
                    if s2 == 0:
                        if sc == 0:
                            w.append(("ve", build["evod"]))
                            w.append(("ve", build["done_ve"]))
                            w.append(("ve", t1_ve[NCH - 1]))
                        else:
                            w.append(("ve", ts4_ve[sc - 1]))
                            w.append(("ac", gel_ac[sc - 1][31]))
                E("pe", (lambda kc=kc, s2=s2: (lambda h: h.matmul(
                    pC[0:4, s2 * 128:(s2 + 1) * 128], evodfm[:, kc, 1:8:2],
                    ln3fm[:, kc, s2 * 128:(s2 + 1) * 128],
                    start=(kc == 0), stop=(kc == 7))))(), waits=w,
                    inc="pe" if (kc == 7 and s2 == 3) else None)
        ts_pe[sc] = sch.cnt["pe"]
        w_ts = [("pe", ts_pe[sc]), ("pc", consts_done)]
        if sc == 0:
            w_ts.append(("pbld", build["done_pwl"]))
            w_ts.append(("pbld2", build["done_pwl2"]))
            w_ts.extend(arena_barrier)
        ts4_ve[sc] = E("ve", lambda h: h.tensor_scalar(
            ts4[:, :], pC[0:4, 0:512], scal_sb[0:4, 0:1], None, ALU.mult),
            waits=w_ts, inc="ve")


    ld_a(0)
    if NCH > 1:
        ld_a(1)
    head_fast_a(0)
    head_slow_a(0)
    pe1_a(0)
    for c in range(NCH):
        if c + 2 < NCH:
            ld_a(c + 2)
        xt = x_t[c % 3]
        x1t = x1o[c % 2]
        qv = qkv_tm[c % 2]
        vb = v_bf[c % 2]
        xcb = xc_bf[c % 2]
        rows = slice(c * 128, (c + 1) * 128)

        for g in range(6):
            half = pB[:, 0:512] if g % 2 == 0 else pB[:, 512:1024]
            for kc in range(8):
                w = []
                if g == 0 and kc == 0:
                    w = [("ve", lnfmev_ac[c])]
                if c == 0 and kc == 0:
                    w.append((f"wq{g}", 16))
                if kc == 0 and (c > 0 or g >= 2):
                    pc_, pg = (c, g - 2) if g >= 2 else (c - 1, g + 4)
                    w.append((qev_sem[pc_][pg], qev[pc_][pg]))
                E("pe", (lambda g=g, kc=kc, half=half: (lambda h: h.matmul(
                    half, ln1fm[:, kc, :], wqkv_sb[:, kc, g * 512:(g + 1) * 512],
                    start=(kc == 0), stop=(kc == 7))))(), waits=w,
                    inc="pe" if kc == 7 else None)
            qkv_pe[c][g] = sch.cnt["pe"]
            w = [("pe", qkv_pe[c][g])]
            if c >= 2 and g <= 1:
                w.append(("ve", t2_ve[c - 2]))
                w.append(("po", ep_po[c - 2]))
            w.append(("ac", rstd1_ac[c]))
            rcol = RSTD1 if c % 2 == 0 else RSTD1B
            qev[c][g] = E("ve", (lambda g=g, half=half, qv=qv, rcol=rcol: (lambda h:
                h.tensor_scalar(qv[:, g * 512:(g + 1) * 512], half,
                                stat[:, rcol:rcol + 1], None, ALU.mult)))(),
                waits=w, inc="ve")
            qev_sem[c][g] = "ve"

        if c == 0 and NCH > 1:
            head_fast_a(1)
        if c + 1 < NCH:
            pe1_a(c + 1)

        w = [(qev_sem[c][g], qev[c][g]) for g in range(6)]
        if c >= 2:
            w.append(("pe", cs_pe[c - 2]))
        vbf_ve[c] = E("ve", (lambda qv=qv, vb=vb: (lambda h: h.tensor_copy(
            vb[:, :].rearrange("p (nh dh) -> p nh dh", nh=NH),
            qv[:, :].rearrange("p (nh gh) -> p nh gh", gh=GH)[:, :, 128:192])))(),
            waits=w, inc="ve")
        w_p = [(qev_sem[c][g], qev[c][g]) for g in range(6)]
        if c >= 1:
            w_p.append(("ve", ess_ve[c - 1]))
        e_p = ep_po[c] = E("pl", (lambda qv=qv: (lambda h: h.tensor_tensor(
            p_sb[:, :].rearrange("p (nh dh) -> p nh dh", nh=NH),
            qv[:, :].rearrange("p (nh gh) -> p nh gh", gh=GH)[:, :, 0:64],
            qv[:, :].rearrange("p (nh gh) -> p nh gh", gh=GH)[:, :, 64:128],
            ALU.mult)))(), waits=w_p, inc="po")
        e_s = E("ve", lambda h: h.tensor_reduce(
            atile[:, A_S * 16:(A_S + 1) * 16],
            p_sb[:, :].rearrange("p (nh dh) -> p nh dh", nh=NH), AX, ALU.add),
            waits=[("po", e_p)], inc="ve")
        ess_ve[c] = e_s
        e_m = E("ve", lambda h: h.tensor_scalar(
            atile[:, A_M * 16:(A_M + 1) * 16], atile[:, A_S * 16:(A_S + 1) * 16],
            0.125, None, ALU.mult), waits=[("ve", e_s)], inc="ve")
        e_mx = E("ve", lambda h: h.tensor_scalar(
            atile[:, A_NM * 16:(A_NM + 1) * 16], atile[:, A_M * 16:(A_M + 1) * 16],
            0.0, None, ALU.max), waits=[("ve", e_m)], inc="ve")
        e_sm = E("ve", lambda h: h.tensor_tensor(
            atile[:, A_SM * 16:(A_SM + 1) * 16], atile[:, A_M * 16:(A_M + 1) * 16],
            atile[:, A_NM * 16:(A_NM + 1) * 16], ALU.subtract),
            waits=[("ve", e_mx)], inc="ve")
        e_nm = E("ve", lambda h: h.tensor_scalar(
            atile[:, A_TT * 16:(A_TT + 1) * 16], atile[:, A_NM * 16:(A_NM + 1) * 16],
            -1.0, None, ALU.mult), waits=[("ve", e_mx)], inc="ve")
        w_u = [("ve", e_sm)]
        if c >= 1:
            w_u.append(("ve", attn_ve[c - 1]))
        e_u = E("ac", lambda h: h.activation(
            atile[:, A_U * 16:(A_U + 1) * 16], atile[:, A_SM * 16:(A_SM + 1) * 16], AF.Exp),
            waits=w_u, inc="ac")
        e_tt = E("ac", lambda h: h.activation(
            atile[:, A_TT * 16:(A_TT + 1) * 16], atile[:, A_TT * 16:(A_TT + 1) * 16], AF.Exp),
            waits=[("ve", e_nm)], inc="ac")
        e_z = E("ve", (lambda c=c: (lambda h: h.scalar_tensor_tensor(
            atile[:, A_Z * 16:(A_Z + 1) * 16], atile[:, A_TT * 16:(A_TT + 1) * 16],
            idx_sb[:, c:c + 1], atile[:, A_U * 16:(A_U + 1) * 16], ALU.mult, ALU.add)))(),
            waits=[("ac", e_u), ("ac", e_tt)], inc="ve")
        e_r = E("ve", lambda h: h.reciprocal(
            atile[:, A_R * 16:(A_R + 1) * 16], atile[:, A_Z * 16:(A_Z + 1) * 16]),
            waits=[("ve", e_z)], inc="ve")
        e_A = E("ve", lambda h: h.tensor_tensor(
            atile[:, A_A * 16:(A_A + 1) * 16], atile[:, A_TT * 16:(A_TT + 1) * 16],
            atile[:, A_R * 16:(A_R + 1) * 16], ALU.mult), waits=[("ve", e_r)], inc="ve")
        e_B = E("ve", lambda h: h.tensor_tensor(
            atile[:, A_B * 16:(A_B + 1) * 16], atile[:, A_U * 16:(A_U + 1) * 16],
            atile[:, A_R * 16:(A_R + 1) * 16], ALU.mult), waits=[("ve", e_r)], inc="ve")

        if c + 1 < NCH:
            head_slow_a(c + 1)
        if c + 2 < NCH:
            head_fast_a(c + 2)

        if NCH >= 28 and not phase_a_only:
            if c >= 20 and c <= 26 and (c - 20) % 2 == 0:
                w1_mms(0, (c - 20) // 2)
            if c >= 18 and c <= 24 and (c - 18) % 2 == 0:
                w1_load_stats(0, (c - 18) // 2)

        for half in range(2):
            cols = slice(half * 512, (half + 1) * 512)
            w = [("ve", carry_ve[c])]
            if c >= 1 and half == 0:
                w.append(("ve", t1_ve[c - 1]))
            E("pe", (lambda cols=cols: (lambda h: h.matmul(
                pC[:, cols], onesr[:, :], carry_b[0:1, cols], start=True, stop=False)))(),
                waits=w)
            E("pe", (lambda cols=cols, vb=vb: (lambda h: h.matmul(
                pC[:, cols], ubf[:, :], vb[:, cols], start=False, stop=True)))(),
                waits=[("ve", vbf_ve[c])] if half == 0 else [],
                inc="pe" if half == 1 else None)
        s_pe[c] = sch.cnt["pe"]
        for half in range(2):
            cols = slice(half * 512, (half + 1) * 512)
            w = []
            if half == 0:
                w = [("ve", carry_ve[c])]
            E("pe", (lambda cols=cols, vb=vb: (lambda h: h.matmul(
                pD[0:1, cols], onesc[:, :], vb[:, cols],
                start=True, stop=False)))(), waits=w)
            E("pe", (lambda cols=cols: (lambda h: h.matmul(
                pD[0:1, cols], onesr[0:1, 0:1], carry_b[0:1, cols],
                start=False, stop=True)))(),
                inc="pe" if half == 1 else None)
        cs_pe[c] = sch.cnt["pe"]

        carry_ve[c + 1] = E("ve", lambda h: h.tensor_copy(carry_b[0:1, :], pD[0:1, 0:1024]),
                            waits=[("pe", cs_pe[c]), ("ve", carry_ve[c])], inc="ve")

        at = attn_s[c % 2]
        tt2 = t2_s[c % 2]
        w_t1 = [("pe", s_pe[c]), ("ve", e_A)]
        if c >= 2:
            w_t1.append(("ac", apply_ac[c - 2]))
            w_t1.append(("po", t3_po[c - 2]))
        t1 = E("ve", (lambda at=at: (lambda h: h.tensor_tensor(
            at[:, :].rearrange("p (nh dh) -> p nh dh", nh=NH),
            pC[:, :].rearrange("p (nh dh) -> p nh dh", nh=NH),
            atile[:, A_A * 16:(A_A + 1) * 16].broadcast_to([128, 16, 64]),
            ALU.mult)))(), waits=w_t1, inc="ve")
        t1_ve[c] = t1
        w_t2 = [("ve", e_B)]
        if c >= 2:
            w_t2.append(("po", x1_po[c - 2]))
            w_t2.append(("ac", apply_ac[c - 2]))
        t2 = E("ve", (lambda qv=qv, tt2=tt2: (lambda h: h.tensor_tensor(
            tt2[:, :].rearrange("p (nh dh) -> p nh dh", nh=NH),
            qv[:, :].rearrange("p (nh gh) -> p nh gh", gh=GH)[:, :, 128:192],
            atile[:, A_B * 16:(A_B + 1) * 16].broadcast_to([128, 16, 64]),
            ALU.mult)))(), waits=w_t2, inc="ve")
        t2_ve[c] = t2
        t3_po[c] = attn_po = E("pl", (lambda at=at, tt2=tt2: (lambda h: h.tensor_tensor(
            at[:, :], at[:, :], tt2[:, :], ALU.add)))(),
            waits=[("ve", t1), ("ve", t2)], inc="po")
        attn_ve[c] = 0

        w_q2 = [("po", attn_po), ("ac", ja_last[0])]
        sq2 = E("ac", (lambda at=at: (lambda h: h.activation(
            junka[:, :], at[:, :], AF.Square,
            accum_out=stat[:, SUMSQ2:SUMSQ2 + 1])))(), waits=w_q2, inc="ac")
        ja_last[0] = sq2
        sum2 = E("ac", (lambda at=at: (lambda h: h.activation(
            junka[:, :], at[:, :], AF.Copy, accum_out=stat[:, SUM2:SUM2 + 1])))(),
            waits=[("po", attn_po), ("ac", ja_last[0])], inc="ac")
        ja_last[0] = sum2
        p_ = E("ac", lambda h: h.activation(
            stat[:, NEGM2:NEGM2 + 1], stat[:, SUM2:SUM2 + 1], AF.Copy,
            scale=cst[:, C_NINV:C_NINV + 1]), waits=[("ac", sum2)], inc="ac")
        p_ = E("ac", lambda h: h.activation(
            stat[:, TMP2:TMP2 + 1], stat[:, NEGM2:NEGM2 + 1], AF.Square),
            waits=[("ac", p_)], inc="ac")
        p_ = E("ac", lambda h: h.activation(
            stat[:, BB2:BB2 + 1], stat[:, TMP2:TMP2 + 1], AF.Identity,
            scale=cst[:, C_NEG1:C_NEG1 + 1], bias=cst[:, C_EPS:C_EPS + 1]),
            waits=[("ac", p_)], inc="ac")
        lnv2_ac[c] = p_ = E("ac", lambda h: h.activation(
            stat[:, LNV2:LNV2 + 1], stat[:, SUMSQ2:SUMSQ2 + 1], AF.Ln,
            scale=cst[:, C_INV:C_INV + 1], bias=stat[:, BB2:BB2 + 1]),
            waits=[("ac", p_)], inc="ac")
        p_ = E("ac", lambda h: h.activation(
            stat[:, RSTD2:RSTD2 + 1], stat[:, LNV2:LNV2 + 1], AF.Exp,
            scale=cst[:, C_NHALF:C_NHALF + 1]), waits=[("ac", p_)], inc="ac")
        p_ = E("ac", lambda h: h.activation(
            stat[:, NMR2:NMR2 + 1], stat[:, NEGM2:NEGM2 + 1], AF.Copy,
            scale=stat[:, RSTD2:RSTD2 + 1]), waits=[("ac", p_)], inc="ac")
        apply_ac[c] = E("ac", (lambda at=at, tt2=tt2: (lambda h: h.activation(
            tt2[:, :], at[:, :], AF.Identity,
            bias=stat[:, NMR2:NMR2 + 1], scale=stat[:, RSTD2:RSTD2 + 1])))(),
            waits=[("ac", p_)], inc="ac")
        w_x1 = [("ac", apply_ac[c])]
        if c >= 2:
            w_x1.append((f"ssw{c - 2}" if c - 2 < 4 else f"sst{c % 2}", stx1[c - 2]))
        x1_po[c] = E("pl", (lambda xt=xt, x1t=x1t, tt2=tt2: (lambda h: h.tensor_tensor(
            x1t[:, :], xt[:, :], tt2[:, :], ALU.add)))(), waits=w_x1, inc="po")
        st_tgt = out_d if phase_a_only else x1s_d
        st_sem = f"ssw{c}" if c < 4 else f"sst{c % 2}"
        stx1[c] = E("sp", (lambda x1t=x1t, rows=rows, st_tgt=st_tgt: (lambda h: h.dma_start(
            st_tgt[rows, :], x1t[:, :])))(), waits=[("po", x1_po[c])], inc=st_sem, by=16)


    arena_barrier = [(f"ssw{c}", 16) for c in range(min(4, NCH))]
    if NCH >= 6:
        arena_barrier += [("sst0", stx1[NCH - 2]), ("sst1", stx1[NCH - 1])]
    elif NCH == 5:
        arena_barrier += [("sst0", stx1[4])]
    arena_barrier += [("po", x1_po[NCH - 1]), ("ac", apply_ac[NCH - 1]),
                      ("ve", carry_ve[NCH]), ("pe", s_pe[NCH - 1])]

    E("pl", lambda h: h.nop(), waits=[("pe", cs_pe[NCH - 1]), ("ve", t2_ve[NCH - 1]), ("po", ep_po[NCH - 1])])
    for j in range(8):
        E("pl", (lambda j=j: (lambda h: h.dma_start(
            w2t_sb[:, 4 * j:4 * (j + 1), :],
            w2t_d[j * 512:(j + 1) * 512, :].rearrange("(kc p) n -> p kc n", p=128))))(),
            inc="pw2", by=16)
    E("pl", lambda h: h.dma_start(b2_bf[:, :], b2r_d),
      waits=list(arena_barrier), inc="pw2", by=16)
    w2_done = b2_done = sch.cnt["pw2"]

    if not (NCH >= 28):
        for s in range(4):
            w1_load_stats(0, s)
            w1_mms(0, s)
    evod_ve = E("ve", lambda h: h.tensor_copy(
        evodfm[:, :, :], ln3fm[:, :, 1:17]),
        waits=[("ac", l3ev[0][0]), ("po", x1_po[NCH - 2]),
               ("ac", apply_ac[NCH - 2])], inc="ve")
    build["evod"] = evod_ve
    E("ve", lambda h: h.memset(r1_sb[:, :], 0.0), inc="ve")
    E("ve", lambda h: h.memset(rs_sb[:, :], 0.0),
      waits=[("po", x1_po[NCH - 2]), ("ac", apply_ac[NCH - 2])], inc="ve")
    E("ve", lambda h: h.memset(odd4ext[:, :, :], 0.0), inc="ve")
    memset_ve = sch.cnt["ve"]
    for kc in range(32):
        p_, cc = kc // 8, kc % 8
        E("ve", (lambda kc=kc, p_=p_, cc=cc: (lambda h: h.tensor_copy(
            odd4ext[:, kc, p_ * 4:(p_ + 1) * 4], evodfm[:, cc, 9:16:2])))(),
            waits=[("ve", evod_ve), ("ve", memset_ve)], inc="ve")
    build["odd4ext"] = sch.cnt["ve"]
    stag_dma = [0, 0]
    stag4_dma = [0]
    ev_slot = [0, 0]
    stags = (stag, stag2)
    ridx = 0
    for i in range(4):
        for src_col, which in ((2 * i, "r1"), (8 + 2 * i, "rs")):
            sg = stags[ridx % 2]
            prow = 32 * (ridx % 2)
            bsem = "pbld" if ridx % 2 == 0 else "pbld2"
            for kc in range(8):
                w = [("ve", evod_ve)] if kc == 0 else []
                if kc == 0:
                    if ridx < 2:
                        w.extend(pa_war[0])
                    if ev_slot[prow // 32]:
                        w.append(("ve", ev_slot[prow // 32]))
                E("pe", (lambda kc=kc, src_col=src_col, prow=prow: (lambda h: h.matmul(
                    pA[prow:prow + 1, kc * 128:(kc + 1) * 128],
                    evodfm[:, kc, src_col:src_col + 1], i128b[:, :],
                    start=True, stop=True)))(), waits=w,
                    inc="pe" if kc == 7 else None)
            row_pe = sch.cnt["pe"]
            w_ev = [("pe", row_pe)]
            if ridx == 0:
                cte = NCH - 2
                w_ev.append(("po", x1_po[cte]))
                w_ev.append((f"ssw{cte}" if cte < 4 else f"sst{cte % 2}",
                             stx1[cte]))
            if ridx == 1:
                lastx0 = ((NCH - 1) // 3) * 3
                w_ev.append(("po", x1_po[lastx0]))
                w_ev.append(("ac", xc_ac[lastx0]))
            if stag_dma[ridx % 2]:
                w_ev.append((bsem, stag_dma[ridx % 2]))
            ev = E("ve", (lambda sg=sg, prow=prow: (lambda h: h.tensor_copy(
                sg[0:1, :], pA[prow:prow + 1, 0:1024])))(), waits=w_ev, inc="ve")
            ev_slot[prow // 32] = ev
            if which == "r1":
                E("sp", (lambda i=i, sg=sg: (lambda h: h.dma_start(
                    r1_sb[i:i + 1, i * 1024:(i + 1) * 1024], sg[0:1, :])))(),
                    waits=[("ve", ev), ("ve", memset_ve)], inc=bsem, by=16)
            else:
                dw = [("ve", ev), ("po", ep_po[NCH - 1])]
                if stag4_dma[0]:
                    dw.append((bsem, stag4_dma[0]))
                for p_ in range(4):
                    E("ve", (lambda p_=p_, sg=sg: (lambda h: h.tensor_copy(
                        stag4[0:1, p_ * 256:(p_ + 1) * 256],
                        sg[0:1, p_:1024:4])))(),
                        waits=dw if p_ == 0 else [], inc="ve")
                deint = sch.cnt["ve"]
                for p_ in range(4):
                    E("sp", (lambda i=i, p_=p_: (lambda h: h.dma_start(
                        rs_sb[p_ * 4 + i:p_ * 4 + i + 1, i * 256:(i + 1) * 256],
                        stag4[0:1, p_ * 256:(p_ + 1) * 256])))(),
                        waits=([("ve", deint), ("ve", memset_ve)]
                               if p_ == 0 else []), inc=bsem, by=16)
                stag4_dma[0] = sch.cnt[bsem]
            stag_dma[ridx % 2] = sch.cnt[bsem]
            ridx += 1
    build["done_pwl"] = sch.cnt["pbld"]
    build["done_pwl2"] = sch.cnt["pbld2"]
    pa_war[0] = [("ve", ev_slot[0]), ("ve", ev_slot[1])]
    rs_scaled = E("ve", lambda h: h.tensor_scalar(
        rs_sb[:, :], rs_sb[:, :], scal_sb[0:16, 0:1], None, ALU.mult),
        waits=[("pbld", build["done_pwl"]), ("pbld2", build["done_pwl2"])], inc="ve")
    build["done_ve"] = rs_scaled
    tst(0)

    for sc in range(NSC):
        hrot = (pB[:, 0:512], pB[:, 512:1024], pC[:, 0:512])
        for hc in range(32):
            hp = hrot[hc % 3]
            for kc in range(8):
                w = []
                if kc == 0:
                    if hc == 0:
                        w = [("ac", l3ev[sc][3]), ("pw1", w1_done), ("ve", ts4_ve[sc])]
                        if sc == 0:
                            w.append(("ve", build["done_ve"]))
                        if sc >= 1:
                            w.append(("ve", fin_ve[sc - 1][2]))
                            w.append(("ac", sum4_ac[sc - 1][2]))
                    if hc == 2:
                        w.append(("ve", ts4_ve[sc]))
                    if hc >= 3:
                        w.append(("ac", gel_ac[sc][hc - 3]))
                E("pe", (lambda kc=kc, hc=hc, hp=hp: (lambda h: h.matmul(
                    hp, w1t_sb[:, kc, hc * 128:(hc + 1) * 128], ln3fm[:, kc, 0:512],
                    start=(kc == 0), stop=False)))(), waits=w)
            w = [("ve", ts4_ve[sc])]
            if sc == 0:
                w.append(("pbld", build["done_pwl"]))
                w.append(("pbld2", build["done_pwl2"]))
            E("pe", (lambda hc=hc, hp=hp: (lambda h: h.matmul(
                hp, r1_sb[:, hc * 128:(hc + 1) * 128], ts4[:, :],
                start=False, stop=True)))(), waits=w, inc="pe")
            h_pe[sc][hc] = sch.cnt["pe"]
            w = [("pe", h_pe[sc][hc])]
            if sc >= 1:
                w.append(("pe", o2_pe[sc - 1][3]))
            gel_ac[sc][hc] = E("ac", (lambda hc=hc, hp=hp: (lambda h: h.activation(
                h_fm[:, hc, :], hp, AF.Gelu, bias=b1c_sb[:, hc:hc + 1])))(),
                waits=w, inc="ac")
            if hc == 19 and sc + 1 < NSC:
                w1_load_stats(sc + 1, 0)
        if sc + 1 < NSC:
            w1_mms(sc + 1, 0)
            w1_load_stats(sc + 1, 1)

        for kc in range(32):
            w = []
            if kc == 0:
                w = [("ac", gel_ac[sc][31]), ("ve", ts4_ve[sc])]
                if sc == 0:
                    w.append(("ve", build["odd4ext"]))
                    w.append(("ve", carry_ve[NCH]))
                else:
                    w.append(("ve", fin_ve[sc - 1][3]))
                    w.append(("ve", gtbf_ve[sc - 1]))
            E("pe", (lambda kc=kc: (lambda h: h.matmul(
                pD[0:16, 0:512], odd4ext[:, kc, :], h_fm[:, kc, 0:512],
                start=(kc == 0), stop=(kc == 31))))(), waits=w,
                inc="pe" if kc == 31 else None)
        gt_pe[sc] = sch.cnt["pe"]
        gtbf_ve[sc] = E("ve", lambda h: h.tensor_copy(gt_bf[:, :], pD[0:16, 0:512]),
                        waits=[("pe", gt_pe[sc])], inc="ve")
        if sc + 1 < NSC:
            w1_mms(sc + 1, 1)
            w1_load_stats(sc + 1, 2)

        for s in range(4):
            cglob = sc * 4 + s
            slot2 = x1i[s % 2]
            w = []
            if s < 2:
                if sc >= 1:
                    w.append((f"so{s % 2}", stout[sc - 1][s + 2]))
                else:
                    w.extend(arena_barrier)
            else:
                w.append((f"so{s % 2}", stout[sc][s - 2]))
            ldb[sc][s] = E("sp", (lambda slot=slot2, cglob=cglob: (lambda h: h.dma_start(
                slot[:, :], x1s_d[cglob * 128:(cglob + 1) * 128, :])))(),
                waits=w, inc=f"sb{s % 2}", by=16)
            pt = pB if s % 2 == 0 else pD
            for kc in range(32):
                for half in range(2):
                    cols = slice(half * 512, (half + 1) * 512)
                    w = []
                    if kc == 0 and half == 0:
                        w = [("pw2", w2_done), ("ve", gtbf_ve[sc])]
                        if s % 2 == 0:
                            w.append(("ac", gel_ac[sc][31]))
                        if s >= 2:
                            w.append(("ve", fin_ve[sc][s - 2]))
                            w.append(("ac", sum4_ac[sc][s - 2]))
                    E("pe", (lambda kc=kc, s=s, pt=pt, cols=cols: (lambda h: h.matmul(
                        pt[:, cols], h_fm[:, kc, s * 128:(s + 1) * 128],
                        w2t_sb[:, kc, cols], start=(kc == 0), stop=False)))(), waits=w)
            for half in range(2):
                cols = slice(half * 512, (half + 1) * 512)
                w = []
                if half == 0 and sc == 0:
                    w = [("ve", build["done_ve"])]
                E("pe", (lambda s=s, pt=pt, cols=cols: (lambda h: h.matmul(
                    pt[:, cols], gt_bf[:, s * 128:(s + 1) * 128], rs_sb[0:16, cols],
                    start=False, stop=False)))(), waits=w)
                E("pe", (lambda pt=pt, cols=cols: (lambda h: h.matmul(
                    pt[:, cols], onesr[:, :], b2_bf[0:1, cols],
                    start=False, stop=True)))(),
                    waits=[("pw2", b2_done)] if half == 0 else [],
                    inc="pe" if half == 1 else None)
            o2_pe[sc][s] = sch.cnt["pe"]

            if sc + 1 < NSC:
                if s == 0:
                    w1_mms(sc + 1, 2)
                    w1_load_stats(sc + 1, 3)
                elif s == 1:
                    w1_mms(sc + 1, 3)
                elif s == 2:
                    tst(sc + 1)

            w_sq4 = [("pe", o2_pe[sc][s])]
            prev_nm4w = (nmr4_ac[sc][s - 1] if s > 0 else
                         (nmr4_ac[sc - 1][3] if sc > 0 else apply_ac[NCH - 1]))
            w_sq4.append(("ac", prev_nm4w))
            w_sq4.append(("ve", jv_last[0]))
            sum4 = E("ve", (lambda pt=pt: (lambda h: h.tensor_scalar(
                junkv[:, :], pt[:, :], 1.0, 0.0, ALU.mult, ALU.add,
                accum_out=stat[:, SUM2:SUM2 + 1])))(), waits=w_sq4, inc="ve")
            jv_last[0] = sum4
            sq4 = E("ac", lambda h: h.activation(
                junka[:, :], junkv[:, :], AF.Square,
                accum_out=stat[:, SUMSQ2:SUMSQ2 + 1]),
                waits=[("ve", sum4), ("ac", ja_last[0])], inc="ac")
            ja_last[0] = sq4
            sum4_ac[sc][s] = sq4
            p_ = E("ac", lambda h: h.activation(
                stat[:, NEGM2:NEGM2 + 1], stat[:, SUM2:SUM2 + 1], AF.Copy,
                scale=cst[:, C_NINV:C_NINV + 1]),
                waits=[("ve", sum4), ("ac", sq4)], inc="ac")
            p_ = E("ac", lambda h: h.activation(
                stat[:, TMP2:TMP2 + 1], stat[:, NEGM2:NEGM2 + 1], AF.Square),
                waits=[("ac", p_)], inc="ac")
            p_ = E("ac", lambda h: h.activation(
                stat[:, BB2:BB2 + 1], stat[:, TMP2:TMP2 + 1], AF.Identity,
                scale=cst[:, C_NEG1:C_NEG1 + 1], bias=cst[:, C_EPS:C_EPS + 1]),
                waits=[("ac", p_)], inc="ac")
            lnv4 = p_ = E("ac", lambda h: h.activation(
                stat[:, LNV2:LNV2 + 1], stat[:, SUMSQ2:SUMSQ2 + 1], AF.Ln,
                scale=cst[:, C_INV:C_INV + 1], bias=stat[:, BB2:BB2 + 1]),
                waits=[("ac", p_)], inc="ac")
            lnv4_ac[sc][s] = lnv4
            p_ = E("ac", lambda h: h.activation(
                stat[:, RSTD2:RSTD2 + 1], stat[:, LNV2:LNV2 + 1], AF.Exp,
                scale=cst[:, C_NHALF:C_NHALF + 1]), waits=[("ac", p_)], inc="ac")
            nmr4_ac[sc][s] = E("ac", lambda h: h.activation(
                stat[:, NMR2:NMR2 + 1], stat[:, NEGM2:NEGM2 + 1], AF.Copy,
                scale=stat[:, RSTD2:RSTD2 + 1]), waits=[("ac", p_)], inc="ac")
            xs = E("ve", (lambda slot=slot2: (lambda h: h.tensor_scalar(
                slot[:, :], slot[:, :], stat[:, NMR2:NMR2 + 1], None, ALU.add)))(),
                waits=[("ac", nmr4_ac[sc][s]), (f"sb{s % 2}", ldb[sc][s])], inc="ve")
            fin_ve[sc][s] = E("ve", (lambda pt=pt, slot=slot2: (lambda h: h.scalar_tensor_tensor(
                slot[:, :], pt[:, :], stat[:, RSTD2:RSTD2 + 1], slot[:, :],
                ALU.mult, ALU.add)))(), waits=[("ve", xs)], inc="ve")
            stout[sc][s] = E("sp", (lambda sc=sc, s=s, slot=slot2: (lambda h: h.dma_start(
                out_d[(sc * 4 + s) * 128:(sc * 4 + s + 1) * 128, :], slot[:, :])))(),
                waits=[("ve", fin_ve[sc][s])], inc=f"so{s % 2}", by=16)

    E("sp", lambda h: h.nop(),
      waits=[("so0", sch.cnt["so0"]), ("so1", sch.cnt["so1"])])
    return _finish(nc, sch)


def _finish(nc, sch):
    from contextlib import ExitStack
    es = ExitStack()
    sems = {s: es.enter_context(nc.semaphore(f"{s}_sem")) for s in Sched.SEMS}
    block = es.enter_context(nc.Block())

    @block.tensor
    def _(h):
        sch.replay("pe", h, sems)

    @block.vector
    def _(h):
        sch.replay("ve", h, sems)

    @block.scalar
    def _(h):
        sch.replay("ac", h, sems)

    @block.sync
    def _(h):
        sch.replay("sp", h, sems)

    @block.gpsimd
    def _(h):
        sch.replay("pl", h, sems)

    es.close()
    return nc


def _host_arrays(S, NCH):
    bfloat = ml_dtypes.bfloat16
    cst = np.zeros((128, 8), dtype=np.float32)
    cst[:, C_EPS] = EPS
    cst[:, C_NINV] = -1.0 / D
    cst[:, C_INV] = 1.0 / D
    cst[:, C_NHALF] = -0.5
    cst[:, C_NEG1] = -1.0
    idx = (np.arange(NCH)[None, :] * 128 + np.arange(128)[:, None]).astype(np.float32)
    ubf = np.triu(np.ones((128, 128), dtype=np.float32), k=1).astype(bfloat)
    i128f = np.eye(128, dtype=np.float32)
    i128b = np.eye(128, dtype=np.float32).astype(bfloat)
    onesr = np.ones((1, 128), dtype=np.float32).astype(bfloat)
    onesc = np.ones((128, 1), dtype=np.float32).astype(bfloat)
    return cst, idx, ubf, i128f, i128b, onesr, onesc


def kernel(L0_attn_x, L0_attn_wqkv, L0_ffn_W1, L0_ffn_b1, L0_ffn_W2, L0_ffn_b2,
           L0_ffn_scale):
    x = np.ascontiguousarray(np.asarray(L0_attn_x, dtype=np.float32))
    B, S, _ = x.shape
    NCH = S // 128
    wqkv = np.ascontiguousarray(np.asarray(L0_attn_wqkv, dtype=np.float32))
    w1t = np.ascontiguousarray(np.asarray(L0_ffn_W1, dtype=np.float32).T)
    w2t = np.ascontiguousarray(np.asarray(L0_ffn_W2, dtype=np.float32).T)
    b1c = np.ascontiguousarray(
        np.asarray(L0_ffn_b1, dtype=np.float32).reshape(H // 128, 128).T)
    b2r = np.ascontiguousarray(np.asarray(L0_ffn_b2, dtype=np.float32).reshape(1, D))
    scal = np.full((128, 1), np.float32(np.asarray(L0_ffn_scale)), dtype=np.float32)
    cst, idx, ubf, i128f, i128b, onesr, onesc = _host_arrays(S, NCH)

    nc = build_nc(S)
    shared = dict(wqkv=wqkv, w1t=w1t, w2t=w2t, b1c=b1c, b2r=b2r, cst=cst, idx=idx,
                  scal=scal, ubf=ubf, i128f=i128f, i128b=i128b, onesr=onesr,
                  onesc=onesc)
    in_maps = [dict(x=np.ascontiguousarray(x[b]), **shared) for b in range(B)]
    res = run_bass_kernel_spmd(nc, in_maps, core_ids=list(range(B)))
    return np.stack([r["out"] for r in res.results], axis=0)


# version 26 (final)
# speedup vs baseline: 1.0291x; 1.0291x over previous
import numpy as np
import ml_dtypes
import concourse.bass as bass
import concourse.mybir as mybir
from concourse.bass_utils import run_bass_kernel_spmd

F32 = mybir.dt.float32
BF16 = mybir.dt.bfloat16
U8 = mybir.dt.uint8
AF = mybir.ActivationFunctionType
ALU = mybir.AluOpType
AX = mybir.AxisListType.X

D = 1024
QKV = 3072
H = 4096
NH, DH, GH = 16, 64, 192
EPS = 1e-5

C_EPS, C_NINV, C_INV, C_NHALF, C_NEG1 = 0, 1, 2, 3, 4
SUM1, SUMSQ1, NEGM1, TMP1, VAR1, LNV1, RSTD1, BB1 = 0, 1, 2, 3, 4, 5, 6, 7
RSTD1B = 17
WSUM, WSUMSQ, WNEGM, WTMP, WBB, WLNV, WRSTD = 19, 20, 21, 22, 23, 24, 25
SUM2, SUMSQ2, NEGM2, TMP2, VAR2, LNV2, RSTD2, NMR2, BB2 = 8, 9, 10, 11, 12, 13, 14, 15, 16
A_S, A_M, A_SM, A_NM, A_U, A_TT, A_Z, A_R, A_A, A_B = range(10)


class Sched:

    ENGS = ("pe", "ve", "ac", "sp", "pl")

    SEMS = ("pe", "ve", "ac", "po", "pc", "pwq", "pw1", "pw2", "pbld",
            "sx0", "sx1", "sx2", "sst0", "sst1", "sb0", "sb1", "sb2", "so0", "so1", "pbld2",
            "ssw0", "ssw1", "ssw2", "ssw3",
            "wq0", "wq1", "wq2", "wq3", "wq4", "wq5")

    def __init__(self):
        self.ops = {e: [] for e in self.ENGS}
        self.cnt = {s: 0 for s in self.SEMS}

    def emit(self, eng, build, waits=(), inc=None, by=1):
        waits = tuple((s, v) for (s, v) in waits if v and v > 0)
        after = None
        if inc is not None:
            self.cnt[inc] += by
            after = self.cnt[inc]
        self.ops[eng].append((build, waits, inc, by))
        return after

    def replay(self, eng, handle, sems):
        seen = {}
        for build, waits, inc, by in self.ops[eng]:
            for s, v in waits:
                if seen.get(s, -1) < v:
                    handle.wait_ge(sems[s], v)
                    seen[s] = v
            ins = build(handle)
            if inc is not None:
                ins.then_inc(sems[inc], by)


def build_nc(S, phase_a_only=False, b_stage=99):
    NCH = S // 128
    NSC = S // 512
    nc = bass.Bass("TRN2", target_bir_lowering=False, debug=False)

    x_d = nc.dram_tensor("x", [S, D], F32, kind="ExternalInput").ap()
    wqkv_d = nc.dram_tensor("wqkv", [D, QKV], F32, kind="ExternalInput").ap()
    w1t_d = nc.dram_tensor("w1t", [D, H], F32, kind="ExternalInput").ap()
    w2t_d = nc.dram_tensor("w2t", [H, D], F32, kind="ExternalInput").ap()
    b1c_d = nc.dram_tensor("b1c", [128, H // 128], F32, kind="ExternalInput").ap()
    b2r_d = nc.dram_tensor("b2r", [1, D], F32, kind="ExternalInput").ap()
    cst_d = nc.dram_tensor("cst", [128, 8], F32, kind="ExternalInput").ap()
    idx_d = nc.dram_tensor("idx", [128, NCH], F32, kind="ExternalInput").ap()
    scal_d = nc.dram_tensor("scal", [128, 1], F32, kind="ExternalInput").ap()
    ubf_d = nc.dram_tensor("ubf", [128, 128], BF16, kind="ExternalInput").ap()
    i128f_d = nc.dram_tensor("i128f", [128, 128], F32, kind="ExternalInput").ap()
    i128b_d = nc.dram_tensor("i128b", [128, 128], BF16, kind="ExternalInput").ap()
    onesr_d = nc.dram_tensor("onesr", [1, 128], BF16, kind="ExternalInput").ap()
    onesc_d = nc.dram_tensor("onesc", [128, 1], BF16, kind="ExternalInput").ap()
    x1s_d = nc.dram_tensor("x1s", [S, D], F32, kind="Internal").ap()
    out_d = nc.dram_tensor("out", [S, D], F32, kind="ExternalOutput").ap()

    wq_union = nc.alloc_sbuf_tensor("wq_union", [128, 65536], U8)
    ubase = nc.lookup_mloc(wq_union).addr
    wqkv_sb = nc.alloc_sbuf_tensor_at("wqkv_sb", [128, 8, QKV], BF16, offset=ubase)
    w2t_sb = nc.alloc_sbuf_tensor_at("w2t_sb", [128, 32, D], BF16, offset=ubase)
    qkv_tm = [nc.alloc_sbuf_tensor_at(f"qkv_tm{i}", [128, QKV], BF16,
                                      offset=ubase + 49152 + i * 6144) for i in range(2)]
    v_bf = [nc.alloc_sbuf_tensor_at(f"vbf{i}", [128, D], BF16,
                                    offset=ubase + 61440 + i * 2048) for i in range(2)]
    w1t_sb = nc.alloc_sbuf_tensor("w1t_sb", [128, 8, H], BF16)
    b1c_sb = nc.alloc_sbuf_tensor("b1c_sb", [128, 32], F32)
    cst = nc.alloc_sbuf_tensor("cst_sb", [128, 8], F32)
    idx_sb = nc.alloc_sbuf_tensor("idx_sb", [128, NCH], F32)
    scal_sb = nc.alloc_sbuf_tensor("scal_sb", [128, 1], F32)
    ubf = nc.alloc_sbuf_tensor("ubf_sb", [128, 128], BF16)
    i128b = nc.alloc_sbuf_tensor("i128b_sb", [128, 128], BF16)
    onesr = nc.alloc_sbuf_tensor("onesr_sb", [1, 128], BF16)
    onesc = nc.alloc_sbuf_tensor("onesc_sb", [128, 1], BF16)
    stat = nc.alloc_sbuf_tensor("stat", [128, 28], F32)
    atile = nc.alloc_sbuf_tensor("atile", [128, 16 * 10], F32)
    junkv = nc.alloc_sbuf_tensor("junkv", [128, 1024], F32)
    junka = nc.alloc_sbuf_tensor("junka", [128, 1024], F32)

    arena = nc.alloc_sbuf_tensor("arena", [128, 70 * 1024], U8)
    abase = nc.lookup_mloc(arena).addr
    off = [0]

    def place(name, shape, dtype):
        per = int(np.prod(shape[1:])) * mybir.dt.size(dtype)
        t = nc.alloc_sbuf_tensor_at(name, shape, dtype, offset=abase + off[0])
        off[0] += (per + 31) & ~31
        return t

    offA_start = off[0]
    x_t = [place(f"x_t{i}", [128, D], F32) for i in range(3)]
    x1o = [place(f"x1o{i}", [128, D], F32) for i in range(2)]
    xc_bf = [place(f"xc{i}", [128, D], BF16) for i in range(2)]
    ln1fm = place("ln1fm", [128, 8, 128], BF16)
    p_sb = place("p_sb", [128, D], F32)
    attn_s = [place(f"attn_s{i}", [128, D], F32) for i in range(2)]
    t2_s = [place(f"t2_s{i}", [128, D], F32) for i in range(2)]
    carry_b = place("carry_b", [1, D], BF16)
    sizeA = off[0] - offA_start

    off[0] = offA_start
    h_fm = place("h_fm", [128, 32, 512], BF16)
    x1i = [place("x1i0", [128, D], F32)]
    b2_bf = place("b2_bf", [1, D], BF16)
    evodfm = place("evodfm", [128, 8, 16], BF16)
    rs_sb = place("rs_sb", [16, D], BF16)
    odd4ext = place("odd4ext", [128, 32, 16], BF16)
    ts4 = place("ts4", [4, 512], BF16)
    gt_bf = place("gt_bf", [16, 512], BF16)
    x1i.append(place("x1i1", [128, D], F32))
    assert off[0] - offA_start <= sizeA + 31, (off[0] - offA_start, sizeA)
    off[0] = offA_start + ((sizeA + 31) & ~31)
    ln3fm = place("ln3fm", [128, 8, 512], BF16)
    x1i.append(place("x1i_w1", [128, D], F32))
    xc3_bf = place("xc3", [128, D], BF16)
    r1_sb = place("r1_sb", [4, H], BF16)
    stag = nc.alloc_sbuf_tensor_at("stag", [1, 1024], BF16,
                                   offset=nc.lookup_mloc(x1o[0]).addr)
    stag2 = nc.alloc_sbuf_tensor_at("stag2", [1, 1024], BF16,
                                    offset=nc.lookup_mloc(h_fm).addr)
    stag4 = nc.alloc_sbuf_tensor_at("stag4", [1, 1024], BF16,
                                    offset=nc.lookup_mloc(p_sb).addr)
    sizeB = off[0] - offA_start
    assert max(sizeA, sizeB) <= 70 * 1024, (sizeA, sizeB)

    pA = nc.alloc_psum_tensor("pA", [128, 1024], F32)
    pB = nc.alloc_psum_tensor("pB", [128, 1024], F32)
    pC = nc.alloc_psum_tensor("pC", [128, 1024], F32)
    pD = nc.alloc_psum_tensor("pD", [128, 1024], F32)

    sch = Sched()
    E = sch.emit
    ja_last = [0]
    jv_last = [0]

    for dst_t, src_t in ((cst, cst_d), (idx_sb, idx_d), (ubf, ubf_d),
                     (i128b, i128b_d), (onesr, onesr_d), (onesc, onesc_d),
                     (scal_sb, scal_d), (b1c_sb, b1c_d)):
        E("pl", (lambda d=dst_t, s=src_t: (lambda h: h.dma_start(d[:, :], s)))(),
          inc="pc", by=16)
    consts_done = sch.cnt["pc"]
    for g in range(6):
        E("pl", (lambda g=g: (lambda h: h.dma_start(
            wqkv_sb[:, :, g * 512:(g + 1) * 512],
            wqkv_d[:, g * 512:(g + 1) * 512].rearrange(
                "(kc p) n -> p kc n", p=128))))(), inc=f"wq{g}", by=16)
    for kc in range(8):
        E("pl", (lambda kc=kc: (lambda h: h.dma_start(
            w1t_sb[:, kc, :], w1t_d[kc * 128:(kc + 1) * 128, :])))(), inc="pw1", by=16)
    w1_done = sch.cnt["pw1"]

    E("ve", lambda h: h.memset(carry_b[:, :], 0.0), inc="ve")
    carry_ready = sch.cnt["ve"]

    ld_x = [0] * NCH
    stx1 = [0] * NCH
    lnfm_pe = [0] * NCH
    lnfmev_ac = [0] * NCH
    qkv_pe = [[0] * 6 for _ in range(NCH)]
    qev = [[0] * 6 for _ in range(NCH)]
    qev_sem = [[None] * 6 for _ in range(NCH)]
    vbf_ve = [0] * NCH
    s_pe = [0] * NCH
    cs_pe = [0] * NCH
    carry_ve = [0] * (NCH + 1)
    carry_ve[0] = carry_ready
    attn_ve = [0] * NCH
    apply_ac = [0] * NCH
    xc_ac = [0] * NCH
    rstd1_ac = [0] * NCH
    sq1_ve = [0] * NCH
    negm1_ac = [0] * NCH
    x1_po = [0] * NCH
    t3_po = [0] * NCH
    t1_ve = [0] * NCH
    t2_ve = [0] * NCH
    ess_ve = [0] * NCH
    ep_po = [0] * NCH
    lnv1_ac = [0] * NCH
    lnv2_ac = [0] * NCH

    def ld_a(c):
        xt = x_t[c % 3]
        rows = slice(c * 128, (c + 1) * 128)
        w = []
        if c >= 3:
            w = [("po", x1_po[c - 3]), ("ac", xc_ac[c - 3]), ("ac", sq1_ve[c - 3])]
        ld_x[c] = E("sp", (lambda xt=xt, rows=rows: (lambda h: h.dma_start(
            xt[:, :], x_d[rows, :])))(), waits=w, inc=f"sx{c % 3}", by=16)

    def head_fast_a(c):
        xt = x_t[c % 3]
        sum1 = E("ac", (lambda xt=xt: (lambda h: h.activation(
            junka[:, :], xt[:, :], AF.Copy, accum_out=stat[:, SUM1:SUM1 + 1])))(),
            waits=[(f"sx{c % 3}", ld_x[c]), ("pc", consts_done), ("ac", ja_last[0])],
            inc="ac")
        ja_last[0] = sum1
        negm1_ac[c] = E("ac", lambda h: h.activation(
            stat[:, NEGM1:NEGM1 + 1], stat[:, SUM1:SUM1 + 1], AF.Copy,
            scale=cst[:, C_NINV:C_NINV + 1]), waits=[("ac", sum1)], inc="ac")
        w_xc = [("ac", negm1_ac[c])]
        if c >= 2:
            w_xc.append(("pe", lnfm_pe[c - 2]))
        xc_ac[c] = E("ac", (lambda xt=xt, xcb=xc_bf[c % 2]: (lambda h: h.activation(
            xcb[:, :], xt[:, :], AF.Identity, bias=stat[:, NEGM1:NEGM1 + 1])))(),
            waits=w_xc, inc="ac")
        w_sq = [("ac", ja_last[0])]
        if c >= 1:
            w_sq.append(("ac", lnv1_ac[c - 1]))
        sq1 = E("ac", (lambda xt=xt: (lambda h: h.activation(
            junka[:, :], xt[:, :], AF.Square,
            accum_out=stat[:, SUMSQ1:SUMSQ1 + 1])))(), waits=w_sq, inc="ac")
        ja_last[0] = sq1
        sq1_ve[c] = sq1

    def head_slow_a(c):
        rcol = RSTD1 if c % 2 == 0 else RSTD1B
        p_ = E("ac", lambda h: h.activation(
            stat[:, TMP1:TMP1 + 1], stat[:, NEGM1:NEGM1 + 1], AF.Square),
            waits=[("ac", negm1_ac[c])], inc="ac")
        p_ = E("ac", lambda h: h.activation(
            stat[:, BB1:BB1 + 1], stat[:, TMP1:TMP1 + 1], AF.Identity,
            scale=cst[:, C_NEG1:C_NEG1 + 1], bias=cst[:, C_EPS:C_EPS + 1]),
            waits=[("ac", p_)], inc="ac")
        w_lnv = [("ac", sq1_ve[c]), ("ac", p_)]
        if c >= 2:
            w_lnv.append(("ve", qev[c - 2][4]))
        lnv1_ac[c] = p_ = E("ac", lambda h: h.activation(
            stat[:, LNV1:LNV1 + 1], stat[:, SUMSQ1:SUMSQ1 + 1], AF.Ln,
            scale=cst[:, C_INV:C_INV + 1], bias=stat[:, BB1:BB1 + 1]),
            waits=w_lnv, inc="ac")
        rstd1_ac[c] = E("ac", (lambda rcol=rcol: (lambda h: h.activation(
            stat[:, rcol:rcol + 1], stat[:, LNV1:LNV1 + 1], AF.Exp,
            scale=cst[:, C_NHALF:C_NHALF + 1])))(), waits=[("ac", p_)], inc="ac")

    def pe1_a(c):
        xcb = xc_bf[c % 2]
        for kc in range(8):
            w = []
            if kc == 0:
                w = [("ac", xc_ac[c])] + list(pa_war[0])
            E("pe", (lambda kc=kc, xcb=xcb: (lambda h: h.matmul(
                pA[:, kc * 128:(kc + 1) * 128], xcb[:, kc * 128:(kc + 1) * 128],
                i128b[:, :], start=True, stop=True)))(), waits=w,
                inc="pe" if kc == 7 else None)
        lnfm_pe[c] = sch.cnt["pe"]
        w = [("pe", lnfm_pe[c])]
        if c >= 1:
            w.append(("pe", qkv_pe[c - 1][5]))
        lnfmev_ac[c] = E("ve", lambda h: h.tensor_copy(
            ln1fm[:, :, :], pA[:, :].rearrange("p (a b) -> p a b", a=8)),
            waits=w, inc="ve")
        pa_war[0] = [("ve", lnfmev_ac[c])]

    pa_war = [[]]
    ldx1 = [[0] * 4 for _ in range(NSC)]
    ldb = [[0] * 4 for _ in range(NSC)]
    l3ev = [[0] * 4 for _ in range(NSC)]
    xc3_ac = [[0] * 4 for _ in range(NSC)]
    sq3_ve = [[0] * 4 for _ in range(NSC)]
    ln3_pe = [[0] * 4 for _ in range(NSC)]
    ts_pe = [0] * NSC
    ts4_ve = [0] * NSC
    gel_ac = [[0] * 32 for _ in range(NSC)]
    gt_pe = [0] * NSC
    gtbf_ve = [0] * NSC
    o2_pe = [[0] * 4 for _ in range(NSC)]
    fin_ve = [[0] * 4 for _ in range(NSC)]
    sum4_ac = [[0] * 4 for _ in range(NSC)]
    stout = [[0] * 4 for _ in range(NSC)]
    nmr4_ac = [[0] * 4 for _ in range(NSC)]
    h_pe = [[0] * 32 for _ in range(NSC)]
    lnv3_ac = [[0] * 4 for _ in range(NSC)]
    lnv4_ac = [[0] * 4 for _ in range(NSC)]
    build = {}

    def w1_load_stats(sc, s):
        cglob = sc * 4 + s
        slot = x1i[2]
        w = []
        if sc == 0:
            w.append((f"ssw{s}", 16))
            if s > 0:
                w.append(("ac", xc3_ac[0][s - 1]))
                w.append(("ve", sq3_ve[0][s - 1]))
        else:
            psc, ps = (sc, s - 1) if s > 0 else (sc - 1, 3)
            w.append(("ac", xc3_ac[psc][ps]))
            w.append(("ve", sq3_ve[psc][ps]))
        ldx1[sc][s] = E("sp", (lambda slot=slot, cglob=cglob: (lambda h: h.dma_start(
            slot[:, :], x1s_d[cglob * 128:(cglob + 1) * 128, :])))(),
            waits=w, inc="sb2", by=16)
        w_sq = [("sb2", ldx1[sc][s]), ("ve", jv_last[0])]
        prev_lnv3 = (lnv3_ac[sc][s - 1] if s > 0 else
                     (lnv3_ac[sc - 1][3] if sc > 0 else 0))
        if prev_lnv3:
            w_sq.append(("ac", prev_lnv3))
        sq3 = E("ve", (lambda slot=slot: (lambda h: h.scalar_tensor_tensor(
            junkv[:, :], slot[:, :], 1.0, slot[:, :], ALU.mult, ALU.mult,
            accum_out=stat[:, WSUMSQ:WSUMSQ + 1])))(), waits=w_sq, inc="ve")
        jv_last[0] = sq3
        sq3_ve[sc][s] = sq3
        w_s3 = [("sb2", ldx1[sc][s]), ("ac", ja_last[0])]
        sum3 = E("ac", (lambda slot=slot: (lambda h: h.activation(
            junka[:, :], slot[:, :], AF.Copy, accum_out=stat[:, WSUM:WSUM + 1])))(),
            waits=w_s3, inc="ac")
        ja_last[0] = sum3
        p_ = E("ac", lambda h: h.activation(
            stat[:, WNEGM:WNEGM + 1], stat[:, WSUM:WSUM + 1], AF.Copy,
            scale=cst[:, C_NINV:C_NINV + 1]), waits=[("ac", sum3)], inc="ac")
        p_ = E("ac", lambda h: h.activation(
            stat[:, WTMP:WTMP + 1], stat[:, WNEGM:WNEGM + 1], AF.Square),
            waits=[("ac", p_)], inc="ac")
        p_ = E("ac", lambda h: h.activation(
            stat[:, WBB:WBB + 1], stat[:, WTMP:WTMP + 1], AF.Identity,
            scale=cst[:, C_NEG1:C_NEG1 + 1], bias=cst[:, C_EPS:C_EPS + 1]),
            waits=[("ac", p_)], inc="ac")
        lnv3_ac[sc][s] = p_ = E("ac", lambda h: h.activation(
            stat[:, WLNV:WLNV + 1], stat[:, WSUMSQ:WSUMSQ + 1], AF.Ln,
            scale=cst[:, C_INV:C_INV + 1], bias=stat[:, WBB:WBB + 1]),
            waits=[("ve", sq3), ("ac", p_)], inc="ac")
        p_ = E("ac", lambda h: h.activation(
            stat[:, WRSTD:WRSTD + 1], stat[:, WLNV:WLNV + 1], AF.Exp,
            scale=cst[:, C_NHALF:C_NHALF + 1]), waits=[("ac", p_)], inc="ac")
        p_ = E("ac", lambda h: h.activation(
            stat[:, WBB:WBB + 1], stat[:, WNEGM:WNEGM + 1], AF.Copy,
            scale=stat[:, WRSTD:WRSTD + 1]), waits=[("ac", p_)], inc="ac")
        w_xc = [("ac", p_)]
        psc, ps = (sc, s - 1) if s > 0 else ((sc - 1, 3) if sc > 0 else (None, None))
        if psc is not None:
            w_xc.append(("pe", ln3_pe[psc][ps]))
        xc3_ac[sc][s] = E("ac", (lambda slot=slot: (lambda h: h.activation(
            xc3_bf[:, :], slot[:, :], AF.Identity, bias=stat[:, WBB:WBB + 1],
            scale=stat[:, WRSTD:WRSTD + 1])))(), waits=w_xc, inc="ac")

    def w1_mms(sc, s):
        for kc in range(8):
            w = []
            if kc == 0:
                w = [("ac", xc3_ac[sc][s])] + list(pa_war[0])
            E("pe", (lambda kc=kc: (lambda h: h.matmul(
                pA[:, kc * 128:(kc + 1) * 128], xc3_bf[:, kc * 128:(kc + 1) * 128],
                i128b[:, :], start=True, stop=True)))(), waits=w,
                inc="pe" if kc == 7 else None)
        ln3_pe[sc][s] = sch.cnt["pe"]
        w = [("pe", ln3_pe[sc][s])]
        if sc >= 1 and s == 0:
            w.append(("pe", h_pe[sc - 1][31]))
        l3ev[sc][s] = E("ac", (lambda s=s: (lambda h: h.copy(
            ln3fm[:, :, s * 128:(s + 1) * 128],
            pA[:, :].rearrange("p (a b) -> p a b", a=8))))(), waits=w, inc="ac")
        pa_war[0] = [("ac", l3ev[sc][s])]

    def tst(sc):
        for s2 in range(4):
            for kc in range(8):
                w = []
                if kc == 0:
                    w = [("ac", l3ev[sc][s2])]
                    if s2 == 0:
                        if sc == 0:
                            w.append(("ve", build["evod"]))
                            w.append(("ve", build["done_ve"]))
                            w.append(("ve", t1_ve[NCH - 1]))
                        else:
                            w.append(("ve", ts4_ve[sc - 1]))
                            w.append(("ac", gel_ac[sc - 1][31]))
                E("pe", (lambda kc=kc, s2=s2: (lambda h: h.matmul(
                    pC[0:4, s2 * 128:(s2 + 1) * 128], evodfm[:, kc, 1:8:2],
                    ln3fm[:, kc, s2 * 128:(s2 + 1) * 128],
                    start=(kc == 0), stop=(kc == 7))))(), waits=w,
                    inc="pe" if (kc == 7 and s2 == 3) else None)
        ts_pe[sc] = sch.cnt["pe"]
        w_ts = [("pe", ts_pe[sc]), ("pc", consts_done)]
        if sc == 0:
            w_ts.append(("pbld", build["done_pwl"]))
            w_ts.append(("pbld2", build["done_pwl2"]))
            w_ts.extend(arena_barrier)
        ts4_ve[sc] = E("ve", lambda h: h.tensor_scalar(
            ts4[:, :], pC[0:4, 0:512], scal_sb[0:4, 0:1], None, ALU.mult),
            waits=w_ts, inc="ve")


    ld_a(0)
    if NCH > 1:
        ld_a(1)
    head_fast_a(0)
    head_slow_a(0)
    pe1_a(0)
    for c in range(NCH):
        if c + 2 < NCH:
            ld_a(c + 2)
        xt = x_t[c % 3]
        x1t = x1o[c % 2]
        qv = qkv_tm[c % 2]
        vb = v_bf[c % 2]
        xcb = xc_bf[c % 2]
        rows = slice(c * 128, (c + 1) * 128)

        for g in range(6):
            half = pB[:, 0:512] if g % 2 == 0 else pB[:, 512:1024]
            for kc in range(8):
                w = []
                if g == 0 and kc == 0:
                    w = [("ve", lnfmev_ac[c])]
                if c == 0 and kc == 0:
                    w.append((f"wq{g}", 16))
                if kc == 0 and (c > 0 or g >= 2):
                    pc_, pg = (c, g - 2) if g >= 2 else (c - 1, g + 4)
                    w.append((qev_sem[pc_][pg], qev[pc_][pg]))
                E("pe", (lambda g=g, kc=kc, half=half: (lambda h: h.matmul(
                    half, ln1fm[:, kc, :], wqkv_sb[:, kc, g * 512:(g + 1) * 512],
                    start=(kc == 0), stop=(kc == 7))))(), waits=w,
                    inc="pe" if kc == 7 else None)
            qkv_pe[c][g] = sch.cnt["pe"]
            w = [("pe", qkv_pe[c][g])]
            if c >= 2 and g <= 1:
                w.append(("po", t2_ve[c - 2]))
                w.append(("po", ep_po[c - 2]))
            w.append(("ac", rstd1_ac[c]))
            rcol = RSTD1 if c % 2 == 0 else RSTD1B
            qev[c][g] = E("ve", (lambda g=g, half=half, qv=qv, rcol=rcol: (lambda h:
                h.tensor_scalar(qv[:, g * 512:(g + 1) * 512], half,
                                stat[:, rcol:rcol + 1], None, ALU.mult)))(),
                waits=w, inc="ve")
            qev_sem[c][g] = "ve"

        if c == 0 and NCH > 1:
            head_fast_a(1)
        if c + 1 < NCH:
            pe1_a(c + 1)

        w = [(qev_sem[c][g], qev[c][g]) for g in range(6)]
        if c >= 2:
            w.append(("pe", cs_pe[c - 2]))
        vbf_ve[c] = E("ve", (lambda qv=qv, vb=vb: (lambda h: h.tensor_copy(
            vb[:, :].rearrange("p (nh dh) -> p nh dh", nh=NH),
            qv[:, :].rearrange("p (nh gh) -> p nh gh", gh=GH)[:, :, 128:192])))(),
            waits=w, inc="ve")
        w_p = [(qev_sem[c][g], qev[c][g]) for g in range(6)]
        if c >= 1:
            w_p.append(("ve", ess_ve[c - 1]))
        e_p = ep_po[c] = E("pl", (lambda qv=qv: (lambda h: h.tensor_tensor(
            p_sb[:, :].rearrange("p (nh dh) -> p nh dh", nh=NH),
            qv[:, :].rearrange("p (nh gh) -> p nh gh", gh=GH)[:, :, 0:64],
            qv[:, :].rearrange("p (nh gh) -> p nh gh", gh=GH)[:, :, 64:128],
            ALU.mult)))(), waits=w_p, inc="po")
        e_s = E("ve", lambda h: h.tensor_reduce(
            atile[:, A_S * 16:(A_S + 1) * 16],
            p_sb[:, :].rearrange("p (nh dh) -> p nh dh", nh=NH), AX, ALU.add),
            waits=[("po", e_p)], inc="ve")
        ess_ve[c] = e_s
        e_m = E("ve", lambda h: h.tensor_scalar(
            atile[:, A_M * 16:(A_M + 1) * 16], atile[:, A_S * 16:(A_S + 1) * 16],
            0.125, None, ALU.mult), waits=[("ve", e_s)], inc="ve")
        e_mx = E("ve", lambda h: h.tensor_scalar(
            atile[:, A_NM * 16:(A_NM + 1) * 16], atile[:, A_M * 16:(A_M + 1) * 16],
            0.0, None, ALU.max), waits=[("ve", e_m)], inc="ve")
        e_sm = E("ve", lambda h: h.tensor_tensor(
            atile[:, A_SM * 16:(A_SM + 1) * 16], atile[:, A_M * 16:(A_M + 1) * 16],
            atile[:, A_NM * 16:(A_NM + 1) * 16], ALU.subtract),
            waits=[("ve", e_mx)], inc="ve")
        e_nm = E("ve", lambda h: h.tensor_scalar(
            atile[:, A_TT * 16:(A_TT + 1) * 16], atile[:, A_NM * 16:(A_NM + 1) * 16],
            -1.0, None, ALU.mult), waits=[("ve", e_mx)], inc="ve")
        w_u = [("ve", e_sm)]
        if c >= 1:
            w_u.append(("ve", attn_ve[c - 1]))
        e_u = E("ac", lambda h: h.activation(
            atile[:, A_U * 16:(A_U + 1) * 16], atile[:, A_SM * 16:(A_SM + 1) * 16], AF.Exp),
            waits=w_u, inc="ac")
        e_tt = E("ac", lambda h: h.activation(
            atile[:, A_TT * 16:(A_TT + 1) * 16], atile[:, A_TT * 16:(A_TT + 1) * 16], AF.Exp),
            waits=[("ve", e_nm)], inc="ac")
        e_z = E("ve", (lambda c=c: (lambda h: h.scalar_tensor_tensor(
            atile[:, A_Z * 16:(A_Z + 1) * 16], atile[:, A_TT * 16:(A_TT + 1) * 16],
            idx_sb[:, c:c + 1], atile[:, A_U * 16:(A_U + 1) * 16], ALU.mult, ALU.add)))(),
            waits=[("ac", e_u), ("ac", e_tt)], inc="ve")
        e_r = E("ve", lambda h: h.reciprocal(
            atile[:, A_R * 16:(A_R + 1) * 16], atile[:, A_Z * 16:(A_Z + 1) * 16]),
            waits=[("ve", e_z)], inc="ve")
        e_A = E("ve", lambda h: h.tensor_tensor(
            atile[:, A_A * 16:(A_A + 1) * 16], atile[:, A_TT * 16:(A_TT + 1) * 16],
            atile[:, A_R * 16:(A_R + 1) * 16], ALU.mult), waits=[("ve", e_r)], inc="ve")
        e_B = E("ve", lambda h: h.tensor_tensor(
            atile[:, A_B * 16:(A_B + 1) * 16], atile[:, A_U * 16:(A_U + 1) * 16],
            atile[:, A_R * 16:(A_R + 1) * 16], ALU.mult), waits=[("ve", e_r)], inc="ve")

        if c + 1 < NCH:
            head_slow_a(c + 1)
        if c + 2 < NCH:
            head_fast_a(c + 2)

        if NCH >= 28 and not phase_a_only:
            if c >= 20 and c <= 26 and (c - 20) % 2 == 0:
                w1_mms(0, (c - 20) // 2)
            if c >= 18 and c <= 24 and (c - 18) % 2 == 0:
                w1_load_stats(0, (c - 18) // 2)

        for half in range(2):
            cols = slice(half * 512, (half + 1) * 512)
            w = [("ve" if c == 0 else "ac", carry_ve[c])]
            if c >= 1 and half == 0:
                w.append(("ve", t1_ve[c - 1]))
            E("pe", (lambda cols=cols: (lambda h: h.matmul(
                pC[:, cols], onesr[:, :], carry_b[0:1, cols], start=True, stop=False)))(),
                waits=w)
            E("pe", (lambda cols=cols, vb=vb: (lambda h: h.matmul(
                pC[:, cols], ubf[:, :], vb[:, cols], start=False, stop=True)))(),
                waits=[("ve", vbf_ve[c])] if half == 0 else [],
                inc="pe" if half == 1 else None)
        s_pe[c] = sch.cnt["pe"]
        for half in range(2):
            cols = slice(half * 512, (half + 1) * 512)
            w = []
            if half == 0:
                w = [("ve" if c == 0 else "ac", carry_ve[c])]
            E("pe", (lambda cols=cols, vb=vb: (lambda h: h.matmul(
                pD[0:1, cols], onesc[:, :], vb[:, cols],
                start=True, stop=False)))(), waits=w)
            E("pe", (lambda cols=cols: (lambda h: h.matmul(
                pD[0:1, cols], onesr[0:1, 0:1], carry_b[0:1, cols],
                start=False, stop=True)))(),
                inc="pe" if half == 1 else None)
        cs_pe[c] = sch.cnt["pe"]

        E("ac", lambda h: h.activation(
            carry_b[0:1, 0:512], pD[0:1, 0:512], AF.Copy),
            waits=[("pe", cs_pe[c])], inc="ac")
        carry_ve[c + 1] = E("ac", lambda h: h.activation(
            carry_b[0:1, 512:1024], pD[0:1, 512:1024], AF.Copy), inc="ac")

        at = attn_s[c % 2]
        tt2 = t2_s[c % 2]
        w_t1 = [("pe", s_pe[c]), ("ve", e_A)]
        if c >= 2:
            w_t1.append(("ac", apply_ac[c - 2]))
            w_t1.append(("po", t3_po[c - 2]))
        t1 = E("ve", (lambda at=at: (lambda h: h.tensor_tensor(
            at[:, :].rearrange("p (nh dh) -> p nh dh", nh=NH),
            pC[:, :].rearrange("p (nh dh) -> p nh dh", nh=NH),
            atile[:, A_A * 16:(A_A + 1) * 16].broadcast_to([128, 16, 64]),
            ALU.mult)))(), waits=w_t1, inc="ve")
        t1_ve[c] = t1
        w_t2 = [("ve", e_B)]
        if c >= 2:
            w_t2.append(("po", x1_po[c - 2]))
            w_t2.append(("ac", apply_ac[c - 2]))
        t2 = E("pl", (lambda qv=qv, tt2=tt2: (lambda h: h.tensor_tensor(
            tt2[:, :].rearrange("p (nh dh) -> p nh dh", nh=NH),
            qv[:, :].rearrange("p (nh gh) -> p nh gh", gh=GH)[:, :, 128:192],
            atile[:, A_B * 16:(A_B + 1) * 16].broadcast_to([128, 16, 64]),
            ALU.mult)))(), waits=w_t2, inc="po")
        t2_ve[c] = t2
        t3_po[c] = attn_po = E("pl", (lambda at=at, tt2=tt2: (lambda h: h.tensor_tensor(
            at[:, :], at[:, :], tt2[:, :], ALU.add)))(),
            waits=[("ve", t1), ("po", t2)], inc="po")
        attn_ve[c] = 0

        w_q2 = [("po", attn_po), ("ac", ja_last[0])]
        sq2 = E("ac", (lambda at=at: (lambda h: h.activation(
            junka[:, :], at[:, :], AF.Square,
            accum_out=stat[:, SUMSQ2:SUMSQ2 + 1])))(), waits=w_q2, inc="ac")
        ja_last[0] = sq2
        sum2 = E("ac", (lambda at=at: (lambda h: h.activation(
            junka[:, :], at[:, :], AF.Copy, accum_out=stat[:, SUM2:SUM2 + 1])))(),
            waits=[("po", attn_po), ("ac", ja_last[0])], inc="ac")
        ja_last[0] = sum2
        p_ = E("ac", lambda h: h.activation(
            stat[:, NEGM2:NEGM2 + 1], stat[:, SUM2:SUM2 + 1], AF.Copy,
            scale=cst[:, C_NINV:C_NINV + 1]), waits=[("ac", sum2)], inc="ac")
        p_ = E("ac", lambda h: h.activation(
            stat[:, TMP2:TMP2 + 1], stat[:, NEGM2:NEGM2 + 1], AF.Square),
            waits=[("ac", p_)], inc="ac")
        p_ = E("ac", lambda h: h.activation(
            stat[:, BB2:BB2 + 1], stat[:, TMP2:TMP2 + 1], AF.Identity,
            scale=cst[:, C_NEG1:C_NEG1 + 1], bias=cst[:, C_EPS:C_EPS + 1]),
            waits=[("ac", p_)], inc="ac")
        lnv2_ac[c] = p_ = E("ac", lambda h: h.activation(
            stat[:, LNV2:LNV2 + 1], stat[:, SUMSQ2:SUMSQ2 + 1], AF.Ln,
            scale=cst[:, C_INV:C_INV + 1], bias=stat[:, BB2:BB2 + 1]),
            waits=[("ac", p_)], inc="ac")
        p_ = E("ac", lambda h: h.activation(
            stat[:, RSTD2:RSTD2 + 1], stat[:, LNV2:LNV2 + 1], AF.Exp,
            scale=cst[:, C_NHALF:C_NHALF + 1]), waits=[("ac", p_)], inc="ac")
        p_ = E("ac", lambda h: h.activation(
            stat[:, NMR2:NMR2 + 1], stat[:, NEGM2:NEGM2 + 1], AF.Copy,
            scale=stat[:, RSTD2:RSTD2 + 1]), waits=[("ac", p_)], inc="ac")
        apply_ac[c] = E("ac", (lambda at=at, tt2=tt2: (lambda h: h.activation(
            tt2[:, :], at[:, :], AF.Identity,
            bias=stat[:, NMR2:NMR2 + 1], scale=stat[:, RSTD2:RSTD2 + 1])))(),
            waits=[("ac", p_)], inc="ac")
        w_x1 = [("ac", apply_ac[c])]
        if c >= 2:
            w_x1.append((f"ssw{c - 2}" if c - 2 < 4 else f"sst{c % 2}", stx1[c - 2]))
        x1_po[c] = E("pl", (lambda xt=xt, x1t=x1t, tt2=tt2: (lambda h: h.tensor_tensor(
            x1t[:, :], xt[:, :], tt2[:, :], ALU.add)))(), waits=w_x1, inc="po")
        st_tgt = out_d if phase_a_only else x1s_d
        st_sem = f"ssw{c}" if c < 4 else f"sst{c % 2}"
        stx1[c] = E("sp", (lambda x1t=x1t, rows=rows, st_tgt=st_tgt: (lambda h: h.dma_start(
            st_tgt[rows, :], x1t[:, :])))(), waits=[("po", x1_po[c])], inc=st_sem, by=16)


    arena_barrier = [(f"ssw{c}", 16) for c in range(min(4, NCH))]
    if NCH >= 6:
        arena_barrier += [("sst0", stx1[NCH - 2]), ("sst1", stx1[NCH - 1])]
    elif NCH == 5:
        arena_barrier += [("sst0", stx1[4])]
    arena_barrier += [("po", x1_po[NCH - 1]), ("ac", apply_ac[NCH - 1]),
                      ("ac", carry_ve[NCH]), ("pe", s_pe[NCH - 1])]

    E("pl", lambda h: h.nop(), waits=[("pe", cs_pe[NCH - 1]), ("po", t2_ve[NCH - 1]), ("po", ep_po[NCH - 1])])
    for j in range(8):
        E("pl", (lambda j=j: (lambda h: h.dma_start(
            w2t_sb[:, 4 * j:4 * (j + 1), :],
            w2t_d[j * 512:(j + 1) * 512, :].rearrange("(kc p) n -> p kc n", p=128))))(),
            inc="pw2", by=16)
    E("pl", lambda h: h.dma_start(b2_bf[:, :], b2r_d),
      waits=list(arena_barrier), inc="pw2", by=16)
    w2_done = b2_done = sch.cnt["pw2"]

    if not (NCH >= 28):
        for s in range(4):
            w1_load_stats(0, s)
            w1_mms(0, s)
    evod_ve = E("ve", lambda h: h.tensor_copy(
        evodfm[:, :, :], ln3fm[:, :, 1:17]),
        waits=[("ac", l3ev[0][0]), ("po", x1_po[NCH - 2]),
               ("ac", apply_ac[NCH - 2])], inc="ve")
    build["evod"] = evod_ve
    E("ve", lambda h: h.memset(r1_sb[:, :], 0.0), inc="ve")
    E("ve", lambda h: h.memset(rs_sb[:, :], 0.0),
      waits=[("po", x1_po[NCH - 2]), ("ac", apply_ac[NCH - 2])], inc="ve")
    E("ve", lambda h: h.memset(odd4ext[:, :, :], 0.0), inc="ve")
    memset_ve = sch.cnt["ve"]
    for kc in range(32):
        p_, cc = kc // 8, kc % 8
        E("ve", (lambda kc=kc, p_=p_, cc=cc: (lambda h: h.tensor_copy(
            odd4ext[:, kc, p_ * 4:(p_ + 1) * 4], evodfm[:, cc, 9:16:2])))(),
            waits=[("ve", evod_ve), ("ve", memset_ve)], inc="ve")
    build["odd4ext"] = sch.cnt["ve"]
    stag_dma = [0, 0]
    stag4_dma = [0]
    ev_slot = [0, 0]
    stags = (stag, stag2)
    ridx = 0
    for i in range(4):
        for src_col, which in ((2 * i, "r1"), (8 + 2 * i, "rs")):
            sg = stags[ridx % 2]
            prow = 32 * (ridx % 2)
            bsem = "pbld" if ridx % 2 == 0 else "pbld2"
            for kc in range(8):
                w = [("ve", evod_ve)] if kc == 0 else []
                if kc == 0:
                    if ridx < 2:
                        w.extend(pa_war[0])
                    if ev_slot[prow // 32]:
                        w.append(("ve", ev_slot[prow // 32]))
                E("pe", (lambda kc=kc, src_col=src_col, prow=prow: (lambda h: h.matmul(
                    pA[prow:prow + 1, kc * 128:(kc + 1) * 128],
                    evodfm[:, kc, src_col:src_col + 1], i128b[:, :],
                    start=True, stop=True)))(), waits=w,
                    inc="pe" if kc == 7 else None)
            row_pe = sch.cnt["pe"]
            w_ev = [("pe", row_pe)]
            if ridx == 0:
                cte = NCH - 2
                w_ev.append(("po", x1_po[cte]))
                w_ev.append((f"ssw{cte}" if cte < 4 else f"sst{cte % 2}",
                             stx1[cte]))
            if ridx == 1:
                lastx0 = ((NCH - 1) // 3) * 3
                w_ev.append(("po", x1_po[lastx0]))
                w_ev.append(("ac", xc_ac[lastx0]))
            if stag_dma[ridx % 2]:
                w_ev.append((bsem, stag_dma[ridx % 2]))
            ev = E("ve", (lambda sg=sg, prow=prow: (lambda h: h.tensor_copy(
                sg[0:1, :], pA[prow:prow + 1, 0:1024])))(), waits=w_ev, inc="ve")
            ev_slot[prow // 32] = ev
            if which == "r1":
                E("sp", (lambda i=i, sg=sg: (lambda h: h.dma_start(
                    r1_sb[i:i + 1, i * 1024:(i + 1) * 1024], sg[0:1, :])))(),
                    waits=[("ve", ev), ("ve", memset_ve)], inc=bsem, by=16)
            else:
                dw = [("ve", ev), ("po", ep_po[NCH - 1])]
                if stag4_dma[0]:
                    dw.append((bsem, stag4_dma[0]))
                for p_ in range(4):
                    E("ve", (lambda p_=p_, sg=sg: (lambda h: h.tensor_copy(
                        stag4[0:1, p_ * 256:(p_ + 1) * 256],
                        sg[0:1, p_:1024:4])))(),
                        waits=dw if p_ == 0 else [], inc="ve")
                deint = sch.cnt["ve"]
                for p_ in range(4):
                    E("sp", (lambda i=i, p_=p_: (lambda h: h.dma_start(
                        rs_sb[p_ * 4 + i:p_ * 4 + i + 1, i * 256:(i + 1) * 256],
                        stag4[0:1, p_ * 256:(p_ + 1) * 256])))(),
                        waits=([("ve", deint), ("ve", memset_ve)]
                               if p_ == 0 else []), inc=bsem, by=16)
                stag4_dma[0] = sch.cnt[bsem]
            stag_dma[ridx % 2] = sch.cnt[bsem]
            ridx += 1
    build["done_pwl"] = sch.cnt["pbld"]
    build["done_pwl2"] = sch.cnt["pbld2"]
    pa_war[0] = [("ve", ev_slot[0]), ("ve", ev_slot[1])]
    rs_scaled = E("ve", lambda h: h.tensor_scalar(
        rs_sb[:, :], rs_sb[:, :], scal_sb[0:16, 0:1], None, ALU.mult),
        waits=[("pbld", build["done_pwl"]), ("pbld2", build["done_pwl2"])], inc="ve")
    build["done_ve"] = rs_scaled
    tst(0)

    for sc in range(NSC):
        hrot = (pB[:, 0:512], pB[:, 512:1024], pC[:, 0:512])
        for hc in range(32):
            hp = hrot[hc % 3]
            for kc in range(8):
                w = []
                if kc == 0:
                    if hc == 0:
                        w = [("ac", l3ev[sc][3]), ("pw1", w1_done), ("ve", ts4_ve[sc])]
                        if sc == 0:
                            w.append(("ve", build["done_ve"]))
                        if sc >= 1:
                            w.append(("ve", fin_ve[sc - 1][2]))
                            w.append(("ac", sum4_ac[sc - 1][2]))
                    if hc == 2:
                        w.append(("ve", ts4_ve[sc]))
                    if hc >= 3:
                        w.append(("ac", gel_ac[sc][hc - 3]))
                E("pe", (lambda kc=kc, hc=hc, hp=hp: (lambda h: h.matmul(
                    hp, w1t_sb[:, kc, hc * 128:(hc + 1) * 128], ln3fm[:, kc, 0:512],
                    start=(kc == 0), stop=False)))(), waits=w)
            w = [("ve", ts4_ve[sc])]
            if sc == 0:
                w.append(("pbld", build["done_pwl"]))
                w.append(("pbld2", build["done_pwl2"]))
            E("pe", (lambda hc=hc, hp=hp: (lambda h: h.matmul(
                hp, r1_sb[:, hc * 128:(hc + 1) * 128], ts4[:, :],
                start=False, stop=True)))(), waits=w, inc="pe")
            h_pe[sc][hc] = sch.cnt["pe"]
            w = [("pe", h_pe[sc][hc])]
            if sc >= 1:
                w.append(("pe", o2_pe[sc - 1][3]))
            gel_ac[sc][hc] = E("ac", (lambda hc=hc, hp=hp: (lambda h: h.activation(
                h_fm[:, hc, :], hp, AF.Gelu, bias=b1c_sb[:, hc:hc + 1])))(),
                waits=w, inc="ac")
            if hc == 19 and sc + 1 < NSC:
                w1_load_stats(sc + 1, 0)
        if sc + 1 < NSC:
            w1_mms(sc + 1, 0)
            w1_load_stats(sc + 1, 1)

        for kc in range(32):
            w = []
            if kc == 0:
                w = [("ac", gel_ac[sc][31]), ("ve", ts4_ve[sc])]
                if sc == 0:
                    w.append(("ve", build["odd4ext"]))
                    w.append(("ac", carry_ve[NCH]))
                else:
                    w.append(("ve", fin_ve[sc - 1][3]))
                    w.append(("ve", gtbf_ve[sc - 1]))
            E("pe", (lambda kc=kc: (lambda h: h.matmul(
                pD[0:16, 0:512], odd4ext[:, kc, :], h_fm[:, kc, 0:512],
                start=(kc == 0), stop=(kc == 31))))(), waits=w,
                inc="pe" if kc == 31 else None)
        gt_pe[sc] = sch.cnt["pe"]
        gtbf_ve[sc] = E("ve", lambda h: h.tensor_copy(gt_bf[:, :], pD[0:16, 0:512]),
                        waits=[("pe", gt_pe[sc])], inc="ve")
        if sc + 1 < NSC:
            w1_mms(sc + 1, 1)
            w1_load_stats(sc + 1, 2)

        for s in range(4):
            cglob = sc * 4 + s
            slot2 = x1i[s % 2]
            w = []
            if s < 2:
                if sc >= 1:
                    w.append((f"so{s % 2}", stout[sc - 1][s + 2]))
                else:
                    w.extend(arena_barrier)
            else:
                w.append((f"so{s % 2}", stout[sc][s - 2]))
            ldb[sc][s] = E("sp", (lambda slot=slot2, cglob=cglob: (lambda h: h.dma_start(
                slot[:, :], x1s_d[cglob * 128:(cglob + 1) * 128, :])))(),
                waits=w, inc=f"sb{s % 2}", by=16)
            pt = pB if s % 2 == 0 else pD
            for kc in range(32):
                for half in range(2):
                    cols = slice(half * 512, (half + 1) * 512)
                    w = []
                    if kc == 0 and half == 0:
                        w = [("pw2", w2_done), ("ve", gtbf_ve[sc])]
                        if s % 2 == 0:
                            w.append(("ac", gel_ac[sc][31]))
                        if s >= 2:
                            w.append(("ve", fin_ve[sc][s - 2]))
                            w.append(("ac", sum4_ac[sc][s - 2]))
                    E("pe", (lambda kc=kc, s=s, pt=pt, cols=cols: (lambda h: h.matmul(
                        pt[:, cols], h_fm[:, kc, s * 128:(s + 1) * 128],
                        w2t_sb[:, kc, cols], start=(kc == 0), stop=False)))(), waits=w)
            for half in range(2):
                cols = slice(half * 512, (half + 1) * 512)
                w = []
                if half == 0 and sc == 0:
                    w = [("ve", build["done_ve"])]
                E("pe", (lambda s=s, pt=pt, cols=cols: (lambda h: h.matmul(
                    pt[:, cols], gt_bf[:, s * 128:(s + 1) * 128], rs_sb[0:16, cols],
                    start=False, stop=False)))(), waits=w)
                E("pe", (lambda pt=pt, cols=cols: (lambda h: h.matmul(
                    pt[:, cols], onesr[:, :], b2_bf[0:1, cols],
                    start=False, stop=True)))(),
                    waits=[("pw2", b2_done)] if half == 0 else [],
                    inc="pe" if half == 1 else None)
            o2_pe[sc][s] = sch.cnt["pe"]

            if sc + 1 < NSC:
                if s == 0:
                    w1_mms(sc + 1, 2)
                    w1_load_stats(sc + 1, 3)
                elif s == 1:
                    w1_mms(sc + 1, 3)
                elif s == 2:
                    tst(sc + 1)

            w_sq4 = [("pe", o2_pe[sc][s])]
            prev_nm4w = (nmr4_ac[sc][s - 1] if s > 0 else
                         (nmr4_ac[sc - 1][3] if sc > 0 else apply_ac[NCH - 1]))
            w_sq4.append(("ac", prev_nm4w))
            w_sq4.append(("ve", jv_last[0]))
            sum4 = E("ve", (lambda pt=pt: (lambda h: h.tensor_scalar(
                junkv[:, :], pt[:, :], 1.0, 0.0, ALU.mult, ALU.add,
                accum_out=stat[:, SUM2:SUM2 + 1])))(), waits=w_sq4, inc="ve")
            jv_last[0] = sum4
            sq4 = E("ac", lambda h: h.activation(
                junka[:, :], junkv[:, :], AF.Square,
                accum_out=stat[:, SUMSQ2:SUMSQ2 + 1]),
                waits=[("ve", sum4), ("ac", ja_last[0])], inc="ac")
            ja_last[0] = sq4
            sum4_ac[sc][s] = sq4
            p_ = E("ac", lambda h: h.activation(
                stat[:, NEGM2:NEGM2 + 1], stat[:, SUM2:SUM2 + 1], AF.Copy,
                scale=cst[:, C_NINV:C_NINV + 1]),
                waits=[("ve", sum4), ("ac", sq4)], inc="ac")
            p_ = E("ac", lambda h: h.activation(
                stat[:, TMP2:TMP2 + 1], stat[:, NEGM2:NEGM2 + 1], AF.Square),
                waits=[("ac", p_)], inc="ac")
            p_ = E("ac", lambda h: h.activation(
                stat[:, BB2:BB2 + 1], stat[:, TMP2:TMP2 + 1], AF.Identity,
                scale=cst[:, C_NEG1:C_NEG1 + 1], bias=cst[:, C_EPS:C_EPS + 1]),
                waits=[("ac", p_)], inc="ac")
            lnv4 = p_ = E("ac", lambda h: h.activation(
                stat[:, LNV2:LNV2 + 1], stat[:, SUMSQ2:SUMSQ2 + 1], AF.Ln,
                scale=cst[:, C_INV:C_INV + 1], bias=stat[:, BB2:BB2 + 1]),
                waits=[("ac", p_)], inc="ac")
            lnv4_ac[sc][s] = lnv4
            p_ = E("ac", lambda h: h.activation(
                stat[:, RSTD2:RSTD2 + 1], stat[:, LNV2:LNV2 + 1], AF.Exp,
                scale=cst[:, C_NHALF:C_NHALF + 1]), waits=[("ac", p_)], inc="ac")
            nmr4_ac[sc][s] = E("ac", lambda h: h.activation(
                stat[:, NMR2:NMR2 + 1], stat[:, NEGM2:NEGM2 + 1], AF.Copy,
                scale=stat[:, RSTD2:RSTD2 + 1]), waits=[("ac", p_)], inc="ac")
            xs = E("ve", (lambda slot=slot2: (lambda h: h.tensor_scalar(
                slot[:, :], slot[:, :], stat[:, NMR2:NMR2 + 1], None, ALU.add)))(),
                waits=[("ac", nmr4_ac[sc][s]), (f"sb{s % 2}", ldb[sc][s])], inc="ve")
            fin_ve[sc][s] = E("ve", (lambda pt=pt, slot=slot2: (lambda h: h.scalar_tensor_tensor(
                slot[:, :], pt[:, :], stat[:, RSTD2:RSTD2 + 1], slot[:, :],
                ALU.mult, ALU.add)))(), waits=[("ve", xs)], inc="ve")
            stout[sc][s] = E("sp", (lambda sc=sc, s=s, slot=slot2: (lambda h: h.dma_start(
                out_d[(sc * 4 + s) * 128:(sc * 4 + s + 1) * 128, :], slot[:, :])))(),
                waits=[("ve", fin_ve[sc][s])], inc=f"so{s % 2}", by=16)

    E("sp", lambda h: h.nop(),
      waits=[("so0", sch.cnt["so0"]), ("so1", sch.cnt["so1"])])
    return _finish(nc, sch)


def _finish(nc, sch):
    from contextlib import ExitStack
    es = ExitStack()
    sems = {s: es.enter_context(nc.semaphore(f"{s}_sem")) for s in Sched.SEMS}
    block = es.enter_context(nc.Block())

    @block.tensor
    def _(h):
        sch.replay("pe", h, sems)

    @block.vector
    def _(h):
        sch.replay("ve", h, sems)

    @block.scalar
    def _(h):
        sch.replay("ac", h, sems)

    @block.sync
    def _(h):
        sch.replay("sp", h, sems)

    @block.gpsimd
    def _(h):
        sch.replay("pl", h, sems)

    es.close()
    return nc


def _host_arrays(S, NCH):
    bfloat = ml_dtypes.bfloat16
    cst = np.zeros((128, 8), dtype=np.float32)
    cst[:, C_EPS] = EPS
    cst[:, C_NINV] = -1.0 / D
    cst[:, C_INV] = 1.0 / D
    cst[:, C_NHALF] = -0.5
    cst[:, C_NEG1] = -1.0
    idx = (np.arange(NCH)[None, :] * 128 + np.arange(128)[:, None]).astype(np.float32)
    ubf = np.triu(np.ones((128, 128), dtype=np.float32), k=1).astype(bfloat)
    i128f = np.eye(128, dtype=np.float32)
    i128b = np.eye(128, dtype=np.float32).astype(bfloat)
    onesr = np.ones((1, 128), dtype=np.float32).astype(bfloat)
    onesc = np.ones((128, 1), dtype=np.float32).astype(bfloat)
    return cst, idx, ubf, i128f, i128b, onesr, onesc


def kernel(L0_attn_x, L0_attn_wqkv, L0_ffn_W1, L0_ffn_b1, L0_ffn_W2, L0_ffn_b2,
           L0_ffn_scale):
    x = np.ascontiguousarray(np.asarray(L0_attn_x, dtype=np.float32))
    B, S, _ = x.shape
    NCH = S // 128
    wqkv = np.ascontiguousarray(np.asarray(L0_attn_wqkv, dtype=np.float32))
    w1t = np.ascontiguousarray(np.asarray(L0_ffn_W1, dtype=np.float32).T)
    w2t = np.ascontiguousarray(np.asarray(L0_ffn_W2, dtype=np.float32).T)
    b1c = np.ascontiguousarray(
        np.asarray(L0_ffn_b1, dtype=np.float32).reshape(H // 128, 128).T)
    b2r = np.ascontiguousarray(np.asarray(L0_ffn_b2, dtype=np.float32).reshape(1, D))
    scal = np.full((128, 1), np.float32(np.asarray(L0_ffn_scale)), dtype=np.float32)
    cst, idx, ubf, i128f, i128b, onesr, onesc = _host_arrays(S, NCH)

    nc = build_nc(S)
    shared = dict(wqkv=wqkv, w1t=w1t, w2t=w2t, b1c=b1c, b2r=b2r, cst=cst, idx=idx,
                  scal=scal, ubf=ubf, i128f=i128f, i128b=i128b, onesr=onesr,
                  onesc=onesc)
    in_maps = [dict(x=np.ascontiguousarray(x[b]), **shared) for b in range(B)]
    res = run_bass_kernel_spmd(nc, in_maps, core_ids=list(range(B)))
    return np.stack([r["out"] for r in res.results], axis=0)


# version 27
# speedup vs baseline: 1.0333x; 1.0041x over previous
import numpy as np
import ml_dtypes
import concourse.bass as bass
import concourse.mybir as mybir
from concourse.bass_utils import run_bass_kernel_spmd

F32 = mybir.dt.float32
BF16 = mybir.dt.bfloat16
U8 = mybir.dt.uint8
AF = mybir.ActivationFunctionType
ALU = mybir.AluOpType
AX = mybir.AxisListType.X

D = 1024
QKV = 3072
H = 4096
NH, DH, GH = 16, 64, 192
EPS = 1e-5

C_EPS, C_NINV, C_INV, C_NHALF, C_NEG1 = 0, 1, 2, 3, 4
SUM1, SUMSQ1, NEGM1, TMP1, VAR1, LNV1, RSTD1, BB1 = 0, 1, 2, 3, 4, 5, 6, 7
RSTD1B = 17
WSUM, WSUMSQ, WNEGM, WTMP, WBB, WLNV, WRSTD = 19, 20, 21, 22, 23, 24, 25
SUM2, SUMSQ2, NEGM2, TMP2, VAR2, LNV2, RSTD2, NMR2, BB2 = 8, 9, 10, 11, 12, 13, 14, 15, 16
A_S, A_M, A_SM, A_NM, A_U, A_TT, A_Z, A_R, A_A, A_B = range(10)


class Sched:

    ENGS = ("pe", "ve", "ac", "sp", "pl")

    SEMS = ("pe", "ve", "ac", "po", "pc", "pwq", "pw1", "pw2", "pbld",
            "sx0", "sx1", "sx2", "sst0", "sst1", "sb0", "sb1", "sb2", "so0", "so1", "pbld2",
            "ssw0", "ssw1", "ssw2", "ssw3",
            "wq0", "wq1", "wq2", "wq3", "wq4", "wq5")

    def __init__(self):
        self.ops = {e: [] for e in self.ENGS}
        self.cnt = {s: 0 for s in self.SEMS}

    def emit(self, eng, build, waits=(), inc=None, by=1):
        waits = tuple((s, v) for (s, v) in waits if v and v > 0)
        after = None
        if inc is not None:
            self.cnt[inc] += by
            after = self.cnt[inc]
        self.ops[eng].append((build, waits, inc, by))
        return after

    def replay(self, eng, handle, sems):
        seen = {}
        for build, waits, inc, by in self.ops[eng]:
            for s, v in waits:
                if seen.get(s, -1) < v:
                    handle.wait_ge(sems[s], v)
                    seen[s] = v
            ins = build(handle)
            if inc is not None:
                ins.then_inc(sems[inc], by)


def build_nc(S, phase_a_only=False, b_stage=99):
    NCH = S // 128
    NSC = S // 512
    nc = bass.Bass("TRN2", target_bir_lowering=False, debug=False)

    x_d = nc.dram_tensor("x", [S, D], F32, kind="ExternalInput").ap()
    wqkv_d = nc.dram_tensor("wqkv", [D, QKV], F32, kind="ExternalInput").ap()
    w1t_d = nc.dram_tensor("w1t", [D, H], F32, kind="ExternalInput").ap()
    w2t_d = nc.dram_tensor("w2t", [H, D], F32, kind="ExternalInput").ap()
    b1c_d = nc.dram_tensor("b1c", [128, H // 128], F32, kind="ExternalInput").ap()
    b2r_d = nc.dram_tensor("b2r", [1, D], F32, kind="ExternalInput").ap()
    cst_d = nc.dram_tensor("cst", [128, 8], F32, kind="ExternalInput").ap()
    idx_d = nc.dram_tensor("idx", [128, NCH], F32, kind="ExternalInput").ap()
    scal_d = nc.dram_tensor("scal", [128, 1], F32, kind="ExternalInput").ap()
    ubf_d = nc.dram_tensor("ubf", [128, 128], BF16, kind="ExternalInput").ap()
    i128f_d = nc.dram_tensor("i128f", [128, 128], F32, kind="ExternalInput").ap()
    i128b_d = nc.dram_tensor("i128b", [128, 128], BF16, kind="ExternalInput").ap()
    onesr_d = nc.dram_tensor("onesr", [1, 128], BF16, kind="ExternalInput").ap()
    onesc_d = nc.dram_tensor("onesc", [128, 1], BF16, kind="ExternalInput").ap()
    x1s_d = nc.dram_tensor("x1s", [S, D], F32, kind="Internal").ap()
    out_d = nc.dram_tensor("out", [S, D], F32, kind="ExternalOutput").ap()

    wq_union = nc.alloc_sbuf_tensor("wq_union", [128, 65536], U8)
    ubase = nc.lookup_mloc(wq_union).addr
    wqkv_sb = nc.alloc_sbuf_tensor_at("wqkv_sb", [128, 8, QKV], BF16, offset=ubase)
    w2t_sb = nc.alloc_sbuf_tensor_at("w2t_sb", [128, 32, D], BF16, offset=ubase)
    qkv_tm = [nc.alloc_sbuf_tensor_at(f"qkv_tm{i}", [128, QKV], BF16,
                                      offset=ubase + 49152 + i * 6144) for i in range(2)]
    v_bf = [nc.alloc_sbuf_tensor_at(f"vbf{i}", [128, D], BF16,
                                    offset=ubase + 61440 + i * 2048) for i in range(2)]
    w1t_sb = nc.alloc_sbuf_tensor("w1t_sb", [128, 8, H], BF16)
    b1c_sb = nc.alloc_sbuf_tensor("b1c_sb", [128, 32], F32)
    cst = nc.alloc_sbuf_tensor("cst_sb", [128, 8], F32)
    idx_sb = nc.alloc_sbuf_tensor("idx_sb", [128, NCH], F32)
    scal_sb = nc.alloc_sbuf_tensor("scal_sb", [128, 1], F32)
    ubf = nc.alloc_sbuf_tensor("ubf_sb", [128, 128], BF16)
    i128b = nc.alloc_sbuf_tensor("i128b_sb", [128, 128], BF16)
    onesr = nc.alloc_sbuf_tensor("onesr_sb", [1, 128], BF16)
    onesc = nc.alloc_sbuf_tensor("onesc_sb", [128, 1], BF16)
    stat = nc.alloc_sbuf_tensor("stat", [128, 28], F32)
    atile = nc.alloc_sbuf_tensor("atile", [128, 16 * 10], F32)
    junkv = nc.alloc_sbuf_tensor("junkv", [128, 1024], F32)
    junka = nc.alloc_sbuf_tensor("junka", [128, 1024], F32)

    arena = nc.alloc_sbuf_tensor("arena", [128, 70 * 1024], U8)
    abase = nc.lookup_mloc(arena).addr
    off = [0]

    def place(name, shape, dtype):
        per = int(np.prod(shape[1:])) * mybir.dt.size(dtype)
        t = nc.alloc_sbuf_tensor_at(name, shape, dtype, offset=abase + off[0])
        off[0] += (per + 31) & ~31
        return t

    offA_start = off[0]
    x_t = [place(f"x_t{i}", [128, D], F32) for i in range(3)]
    x1o = [place(f"x1o{i}", [128, D], F32) for i in range(2)]
    xc_bf = [place(f"xc{i}", [128, D], BF16) for i in range(2)]
    ln1fm = place("ln1fm", [128, 8, 128], BF16)
    p_sb = place("p_sb", [128, D], F32)
    attn_s = [place(f"attn_s{i}", [128, D], F32) for i in range(2)]
    t2_s = [place(f"t2_s{i}", [128, D], F32) for i in range(2)]
    carry_b = place("carry_b", [1, D], BF16)
    sizeA = off[0] - offA_start

    off[0] = offA_start
    h_fm = place("h_fm", [128, 32, 512], BF16)
    x1i = [place("x1i0", [128, D], F32)]
    b2_bf = place("b2_bf", [1, D], BF16)
    evodfm = place("evodfm", [128, 8, 16], BF16)
    rs_sb = place("rs_sb", [16, D], BF16)
    odd4ext = place("odd4ext", [128, 32, 16], BF16)
    ts4 = place("ts4", [4, 512], BF16)
    gt_bf = place("gt_bf", [16, 512], BF16)
    x1i.append(place("x1i1", [128, D], F32))
    assert off[0] - offA_start <= sizeA + 31, (off[0] - offA_start, sizeA)
    off[0] = offA_start + ((sizeA + 31) & ~31)
    ln3fm = place("ln3fm", [128, 8, 512], BF16)
    x1i.append(place("x1i_w1", [128, D], F32))
    xc3_bf = place("xc3", [128, D], BF16)
    r1_sb = place("r1_sb", [4, H], BF16)
    stag = nc.alloc_sbuf_tensor_at("stag", [1, 1024], BF16,
                                   offset=nc.lookup_mloc(x1o[0]).addr)
    stag2 = nc.alloc_sbuf_tensor_at("stag2", [1, 1024], BF16,
                                    offset=nc.lookup_mloc(h_fm).addr)
    stag4 = nc.alloc_sbuf_tensor_at("stag4", [1, 1024], BF16,
                                    offset=nc.lookup_mloc(p_sb).addr)
    sizeB = off[0] - offA_start
    assert max(sizeA, sizeB) <= 70 * 1024, (sizeA, sizeB)

    pA = nc.alloc_psum_tensor("pA", [128, 1024], F32)
    pB = nc.alloc_psum_tensor("pB", [128, 1024], F32)
    pC = nc.alloc_psum_tensor("pC", [128, 1024], F32)
    pD = nc.alloc_psum_tensor("pD", [128, 1024], F32)

    sch = Sched()
    E = sch.emit
    ja_last = [0]
    jv_last = [0]

    for dst_t, src_t in ((cst, cst_d), (idx_sb, idx_d), (ubf, ubf_d),
                     (i128b, i128b_d), (onesr, onesr_d), (onesc, onesc_d),
                     (scal_sb, scal_d), (b1c_sb, b1c_d)):
        E("pl", (lambda d=dst_t, s=src_t: (lambda h: h.dma_start(d[:, :], s)))(),
          inc="pc", by=16)
    consts_done = sch.cnt["pc"]
    for g in range(6):
        E("pl", (lambda g=g: (lambda h: h.dma_start(
            wqkv_sb[:, :, g * 512:(g + 1) * 512],
            wqkv_d[:, g * 512:(g + 1) * 512].rearrange(
                "(kc p) n -> p kc n", p=128))))(), inc=f"wq{g}", by=16)
    for kc in range(8):
        E("pl", (lambda kc=kc: (lambda h: h.dma_start(
            w1t_sb[:, kc, :], w1t_d[kc * 128:(kc + 1) * 128, :])))(), inc="pw1", by=16)
    w1_done = sch.cnt["pw1"]

    E("ve", lambda h: h.memset(carry_b[:, :], 0.0), inc="ve")
    carry_ready = sch.cnt["ve"]

    ld_x = [0] * NCH
    stx1 = [0] * NCH
    lnfm_pe = [0] * NCH
    lnfmev_ac = [0] * NCH
    qkv_pe = [[0] * 6 for _ in range(NCH)]
    qev = [[0] * 6 for _ in range(NCH)]
    qev_sem = [[None] * 6 for _ in range(NCH)]
    vbf_ve = [0] * NCH
    s_pe = [0] * NCH
    cs_pe = [0] * NCH
    carry_ve = [0] * (NCH + 1)
    carry_ve[0] = carry_ready
    attn_ve = [0] * NCH
    apply_ac = [0] * NCH
    xc_ac = [0] * NCH
    rstd1_ac = [0] * NCH
    sq1_ve = [0] * NCH
    negm1_ac = [0] * NCH
    x1_po = [0] * NCH
    t3_po = [0] * NCH
    t1_ve = [0] * NCH
    t2_ve = [0] * NCH
    ess_ve = [0] * NCH
    ep_po = [0] * NCH
    lnv1_ac = [0] * NCH
    lnv2_ac = [0] * NCH

    def ld_a(c):
        xt = x_t[c % 3]
        rows = slice(c * 128, (c + 1) * 128)
        w = []
        if c >= 3:
            w = [("po", x1_po[c - 3]), ("ac", xc_ac[c - 3]), ("ac", sq1_ve[c - 3])]
        ld_x[c] = E("sp", (lambda xt=xt, rows=rows: (lambda h: h.dma_start(
            xt[:, :], x_d[rows, :])))(), waits=w, inc=f"sx{c % 3}", by=16)

    def head_fast_a(c):
        xt = x_t[c % 3]
        sum1 = E("ac", (lambda xt=xt: (lambda h: h.activation(
            junka[:, :], xt[:, :], AF.Copy, accum_out=stat[:, SUM1:SUM1 + 1])))(),
            waits=[(f"sx{c % 3}", ld_x[c]), ("pc", consts_done), ("ac", ja_last[0])],
            inc="ac")
        ja_last[0] = sum1
        negm1_ac[c] = E("ac", lambda h: h.activation(
            stat[:, NEGM1:NEGM1 + 1], stat[:, SUM1:SUM1 + 1], AF.Copy,
            scale=cst[:, C_NINV:C_NINV + 1]), waits=[("ac", sum1)], inc="ac")
        w_xc = [("ac", negm1_ac[c])]
        if c >= 2:
            w_xc.append(("pe", lnfm_pe[c - 2]))
        xc_ac[c] = E("ac", (lambda xt=xt, xcb=xc_bf[c % 2]: (lambda h: h.activation(
            xcb[:, :], xt[:, :], AF.Identity, bias=stat[:, NEGM1:NEGM1 + 1])))(),
            waits=w_xc, inc="ac")
        w_sq = [("ac", ja_last[0])]
        if c >= 1:
            w_sq.append(("ac", lnv1_ac[c - 1]))
        sq1 = E("ac", (lambda xt=xt: (lambda h: h.activation(
            junka[:, :], xt[:, :], AF.Square,
            accum_out=stat[:, SUMSQ1:SUMSQ1 + 1])))(), waits=w_sq, inc="ac")
        ja_last[0] = sq1
        sq1_ve[c] = sq1

    def head_slow_a(c):
        rcol = RSTD1 if c % 2 == 0 else RSTD1B
        p_ = E("ac", lambda h: h.activation(
            stat[:, TMP1:TMP1 + 1], stat[:, NEGM1:NEGM1 + 1], AF.Square),
            waits=[("ac", negm1_ac[c])], inc="ac")
        p_ = E("ac", lambda h: h.activation(
            stat[:, BB1:BB1 + 1], stat[:, TMP1:TMP1 + 1], AF.Identity,
            scale=cst[:, C_NEG1:C_NEG1 + 1], bias=cst[:, C_EPS:C_EPS + 1]),
            waits=[("ac", p_)], inc="ac")
        w_lnv = [("ac", sq1_ve[c]), ("ac", p_)]
        if c >= 2:
            w_lnv.append(("ve", qev[c - 2][4]))
        lnv1_ac[c] = p_ = E("ac", lambda h: h.activation(
            stat[:, LNV1:LNV1 + 1], stat[:, SUMSQ1:SUMSQ1 + 1], AF.Ln,
            scale=cst[:, C_INV:C_INV + 1], bias=stat[:, BB1:BB1 + 1]),
            waits=w_lnv, inc="ac")
        rstd1_ac[c] = E("ac", (lambda rcol=rcol: (lambda h: h.activation(
            stat[:, rcol:rcol + 1], stat[:, LNV1:LNV1 + 1], AF.Exp,
            scale=cst[:, C_NHALF:C_NHALF + 1])))(), waits=[("ac", p_)], inc="ac")

    def pe1_a(c):
        xcb = xc_bf[c % 2]
        for kc in range(8):
            w = []
            if kc == 0:
                w = [("ac", xc_ac[c])] + list(pa_war[0])
            E("pe", (lambda kc=kc, xcb=xcb: (lambda h: h.matmul(
                pA[:, kc * 128:(kc + 1) * 128], xcb[:, kc * 128:(kc + 1) * 128],
                i128b[:, :], start=True, stop=True)))(), waits=w,
                inc="pe" if kc == 7 else None)
        lnfm_pe[c] = sch.cnt["pe"]
        w = [("pe", lnfm_pe[c])]
        if c >= 1:
            w.append(("pe", qkv_pe[c - 1][5]))
        lnfmev_ac[c] = E("ve", lambda h: h.tensor_copy(
            ln1fm[:, :, :], pA[:, :].rearrange("p (a b) -> p a b", a=8)),
            waits=w, inc="ve")
        pa_war[0] = [("ve", lnfmev_ac[c])]

    pa_war = [[]]
    ldx1 = [[0] * 4 for _ in range(NSC)]
    ldb = [[0] * 4 for _ in range(NSC)]
    l3ev = [[0] * 4 for _ in range(NSC)]
    xc3_ac = [[0] * 4 for _ in range(NSC)]
    sq3_ve = [[0] * 4 for _ in range(NSC)]
    ln3_pe = [[0] * 4 for _ in range(NSC)]
    ts_pe = [0] * NSC
    ts4_ve = [0] * NSC
    gel_ac = [[0] * 32 for _ in range(NSC)]
    gt_pe = [0] * NSC
    gtbf_ve = [0] * NSC
    o2_pe = [[0] * 4 for _ in range(NSC)]
    fin_ve = [[0] * 4 for _ in range(NSC)]
    sum4_ac = [[0] * 4 for _ in range(NSC)]
    stout = [[0] * 4 for _ in range(NSC)]
    nmr4_ac = [[0] * 4 for _ in range(NSC)]
    h_pe = [[0] * 32 for _ in range(NSC)]
    lnv3_ac = [[0] * 4 for _ in range(NSC)]
    lnv4_ac = [[0] * 4 for _ in range(NSC)]
    build = {}

    def w1_load_stats(sc, s):
        cglob = sc * 4 + s
        slot = x1i[2]
        w = []
        if sc == 0:
            w.append((f"ssw{s}", 16))
            if s > 0:
                w.append(("ac", xc3_ac[0][s - 1]))
                w.append(("ve", sq3_ve[0][s - 1]))
        else:
            psc, ps = (sc, s - 1) if s > 0 else (sc - 1, 3)
            w.append(("ac", xc3_ac[psc][ps]))
            w.append(("ve", sq3_ve[psc][ps]))
        ldx1[sc][s] = E("sp", (lambda slot=slot, cglob=cglob: (lambda h: h.dma_start(
            slot[:, :], x1s_d[cglob * 128:(cglob + 1) * 128, :])))(),
            waits=w, inc="sb2", by=16)
        w_sq = [("sb2", ldx1[sc][s]), ("ve", jv_last[0])]
        prev_lnv3 = (lnv3_ac[sc][s - 1] if s > 0 else
                     (lnv3_ac[sc - 1][3] if sc > 0 else 0))
        if prev_lnv3:
            w_sq.append(("ac", prev_lnv3))
        sq3 = E("ve", (lambda slot=slot: (lambda h: h.scalar_tensor_tensor(
            junkv[:, :], slot[:, :], 1.0, slot[:, :], ALU.mult, ALU.mult,
            accum_out=stat[:, WSUMSQ:WSUMSQ + 1])))(), waits=w_sq, inc="ve")
        jv_last[0] = sq3
        sq3_ve[sc][s] = sq3
        w_s3 = [("sb2", ldx1[sc][s]), ("ac", ja_last[0])]
        sum3 = E("ac", (lambda slot=slot: (lambda h: h.activation(
            junka[:, :], slot[:, :], AF.Copy, accum_out=stat[:, WSUM:WSUM + 1])))(),
            waits=w_s3, inc="ac")
        ja_last[0] = sum3
        p_ = E("ac", lambda h: h.activation(
            stat[:, WNEGM:WNEGM + 1], stat[:, WSUM:WSUM + 1], AF.Copy,
            scale=cst[:, C_NINV:C_NINV + 1]), waits=[("ac", sum3)], inc="ac")
        p_ = E("ac", lambda h: h.activation(
            stat[:, WTMP:WTMP + 1], stat[:, WNEGM:WNEGM + 1], AF.Square),
            waits=[("ac", p_)], inc="ac")
        p_ = E("ac", lambda h: h.activation(
            stat[:, WBB:WBB + 1], stat[:, WTMP:WTMP + 1], AF.Identity,
            scale=cst[:, C_NEG1:C_NEG1 + 1], bias=cst[:, C_EPS:C_EPS + 1]),
            waits=[("ac", p_)], inc="ac")
        lnv3_ac[sc][s] = p_ = E("ac", lambda h: h.activation(
            stat[:, WLNV:WLNV + 1], stat[:, WSUMSQ:WSUMSQ + 1], AF.Ln,
            scale=cst[:, C_INV:C_INV + 1], bias=stat[:, WBB:WBB + 1]),
            waits=[("ve", sq3), ("ac", p_)], inc="ac")
        p_ = E("ac", lambda h: h.activation(
            stat[:, WRSTD:WRSTD + 1], stat[:, WLNV:WLNV + 1], AF.Exp,
            scale=cst[:, C_NHALF:C_NHALF + 1]), waits=[("ac", p_)], inc="ac")
        p_ = E("ac", lambda h: h.activation(
            stat[:, WBB:WBB + 1], stat[:, WNEGM:WNEGM + 1], AF.Copy,
            scale=stat[:, WRSTD:WRSTD + 1]), waits=[("ac", p_)], inc="ac")
        w_xc = [("ac", p_)]
        psc, ps = (sc, s - 1) if s > 0 else ((sc - 1, 3) if sc > 0 else (None, None))
        if psc is not None:
            w_xc.append(("pe", ln3_pe[psc][ps]))
        xc3_ac[sc][s] = E("ac", (lambda slot=slot: (lambda h: h.activation(
            xc3_bf[:, :], slot[:, :], AF.Identity, bias=stat[:, WBB:WBB + 1],
            scale=stat[:, WRSTD:WRSTD + 1])))(), waits=w_xc, inc="ac")

    def w1_mms(sc, s):
        for kc in range(8):
            w = []
            if kc == 0:
                w = [("ac", xc3_ac[sc][s])] + list(pa_war[0])
            E("pe", (lambda kc=kc: (lambda h: h.matmul(
                pA[:, kc * 128:(kc + 1) * 128], xc3_bf[:, kc * 128:(kc + 1) * 128],
                i128b[:, :], start=True, stop=True)))(), waits=w,
                inc="pe" if kc == 7 else None)
        ln3_pe[sc][s] = sch.cnt["pe"]
        w = [("pe", ln3_pe[sc][s])]
        if sc >= 1 and s == 0:
            w.append(("pe", h_pe[sc - 1][31]))
        l3ev[sc][s] = E("ac", (lambda s=s: (lambda h: h.copy(
            ln3fm[:, :, s * 128:(s + 1) * 128],
            pA[:, :].rearrange("p (a b) -> p a b", a=8))))(), waits=w, inc="ac")
        pa_war[0] = [("ac", l3ev[sc][s])]

    def tst(sc):
        for s2 in range(4):
            for kc in range(8):
                w = []
                if kc == 0:
                    w = [("ac", l3ev[sc][s2])]
                    if s2 == 0:
                        if sc == 0:
                            w.append(("ve", build["evod"]))
                            w.append(("ve", build["done_ve"]))
                            w.append(("ve", t1_ve[NCH - 1]))
                        else:
                            w.append(("ve", ts4_ve[sc - 1]))
                            w.append(("ac", gel_ac[sc - 1][31]))
                E("pe", (lambda kc=kc, s2=s2: (lambda h: h.matmul(
                    pC[0:4, s2 * 128:(s2 + 1) * 128], evodfm[:, kc, 1:8:2],
                    ln3fm[:, kc, s2 * 128:(s2 + 1) * 128],
                    start=(kc == 0), stop=(kc == 7))))(), waits=w,
                    inc="pe" if (kc == 7 and s2 == 3) else None)
        ts_pe[sc] = sch.cnt["pe"]
        w_ts = [("pe", ts_pe[sc]), ("pc", consts_done)]
        if sc == 0:
            w_ts.append(("pbld", build["done_pwl"]))
            w_ts.append(("pbld2", build["done_pwl2"]))
            w_ts.extend(arena_barrier)
        ts4_ve[sc] = E("ve", lambda h: h.tensor_scalar(
            ts4[:, :], pC[0:4, 0:512], scal_sb[0:4, 0:1], None, ALU.mult),
            waits=w_ts, inc="ve")


    ld_a(0)
    if NCH > 1:
        ld_a(1)
    head_fast_a(0)
    head_slow_a(0)
    pe1_a(0)
    for c in range(NCH):
        if c + 2 < NCH:
            ld_a(c + 2)
        xt = x_t[c % 3]
        x1t = x1o[c % 2]
        qv = qkv_tm[c % 2]
        vb = v_bf[c % 2]
        xcb = xc_bf[c % 2]
        rows = slice(c * 128, (c + 1) * 128)

        for g in range(6):
            half = pB[:, 0:512] if g % 2 == 0 else pB[:, 512:1024]
            for kc in range(8):
                w = []
                if g == 0 and kc == 0:
                    w = [("ve", lnfmev_ac[c])]
                if c == 0 and kc == 0:
                    w.append((f"wq{g}", 16))
                if kc == 0 and (c > 0 or g >= 2):
                    pc_, pg = (c, g - 2) if g >= 2 else (c - 1, g + 4)
                    w.append((qev_sem[pc_][pg], qev[pc_][pg]))
                E("pe", (lambda g=g, kc=kc, half=half: (lambda h: h.matmul(
                    half, ln1fm[:, kc, :], wqkv_sb[:, kc, g * 512:(g + 1) * 512],
                    start=(kc == 0), stop=(kc == 7))))(), waits=w,
                    inc="pe" if kc == 7 else None)
            qkv_pe[c][g] = sch.cnt["pe"]
            w = [("pe", qkv_pe[c][g])]
            if c >= 2 and g <= 1:
                w.append(("po", t2_ve[c - 2]))
                w.append(("po", ep_po[c - 2]))
            w.append(("ac", rstd1_ac[c]))
            rcol = RSTD1 if c % 2 == 0 else RSTD1B
            qev[c][g] = E("ve", (lambda g=g, half=half, qv=qv, rcol=rcol: (lambda h:
                h.tensor_scalar(qv[:, g * 512:(g + 1) * 512], half,
                                stat[:, rcol:rcol + 1], None, ALU.mult)))(),
                waits=w, inc="ve")
            qev_sem[c][g] = "ve"

        if c == 0 and NCH > 1:
            head_fast_a(1)
        if c + 1 < NCH:
            pe1_a(c + 1)

        w = [(qev_sem[c][g], qev[c][g]) for g in range(6)]
        if c >= 2:
            w.append(("pe", cs_pe[c - 2]))
        vbf_ve[c] = E("ve", (lambda qv=qv, vb=vb: (lambda h: h.tensor_copy(
            vb[:, :].rearrange("p (nh dh) -> p nh dh", nh=NH),
            qv[:, :].rearrange("p (nh gh) -> p nh gh", gh=GH)[:, :, 128:192])))(),
            waits=w, inc="ve")
        es_half = [0, 0]
        for hh in range(2):
            w_p = [(qev_sem[c][g], qev[c][g]) for g in range(3 * hh, 3 * hh + 3)]
            if hh == 0 and c >= 1:
                w_p.append(("ve", ess_ve[c - 1]))
            p_h = E("pl", (lambda qv=qv, hh=hh: (lambda h: h.tensor_tensor(
                p_sb[:, hh * 512:(hh + 1) * 512].rearrange(
                    "p (nh dh) -> p nh dh", dh=64),
                qv[:, :].rearrange("p (nh gh) -> p nh gh", gh=GH)[:, 8 * hh:8 * hh + 8, 0:64],
                qv[:, :].rearrange("p (nh gh) -> p nh gh", gh=GH)[:, 8 * hh:8 * hh + 8, 64:128],
                ALU.mult)))(), waits=w_p, inc="po")
            if hh == 1:
                ep_po[c] = p_h
            es_half[hh] = E("ve", (lambda hh=hh: (lambda h: h.tensor_reduce(
                atile[:, A_S * 16 + 8 * hh:A_S * 16 + 8 * hh + 8],
                p_sb[:, hh * 512:(hh + 1) * 512].rearrange(
                    "p (nh dh) -> p nh dh", dh=64), AX, ALU.add)))(),
                waits=[("po", p_h)], inc="ve")
        e_s = es_half[1]
        ess_ve[c] = e_s
        e_m = E("ve", lambda h: h.tensor_scalar(
            atile[:, A_M * 16:(A_M + 1) * 16], atile[:, A_S * 16:(A_S + 1) * 16],
            0.125, None, ALU.mult), waits=[("ve", e_s)], inc="ve")
        e_mx = E("ve", lambda h: h.tensor_scalar(
            atile[:, A_NM * 16:(A_NM + 1) * 16], atile[:, A_M * 16:(A_M + 1) * 16],
            0.0, None, ALU.max), waits=[("ve", e_m)], inc="ve")
        e_sm = E("ve", lambda h: h.tensor_tensor(
            atile[:, A_SM * 16:(A_SM + 1) * 16], atile[:, A_M * 16:(A_M + 1) * 16],
            atile[:, A_NM * 16:(A_NM + 1) * 16], ALU.subtract),
            waits=[("ve", e_mx)], inc="ve")
        e_nm = E("ve", lambda h: h.tensor_scalar(
            atile[:, A_TT * 16:(A_TT + 1) * 16], atile[:, A_NM * 16:(A_NM + 1) * 16],
            -1.0, None, ALU.mult), waits=[("ve", e_mx)], inc="ve")
        w_u = [("ve", e_sm)]
        if c >= 1:
            w_u.append(("ve", attn_ve[c - 1]))
        e_u = E("ac", lambda h: h.activation(
            atile[:, A_U * 16:(A_U + 1) * 16], atile[:, A_SM * 16:(A_SM + 1) * 16], AF.Exp),
            waits=w_u, inc="ac")
        e_tt = E("ac", lambda h: h.activation(
            atile[:, A_TT * 16:(A_TT + 1) * 16], atile[:, A_TT * 16:(A_TT + 1) * 16], AF.Exp),
            waits=[("ve", e_nm)], inc="ac")
        e_z = E("ve", (lambda c=c: (lambda h: h.scalar_tensor_tensor(
            atile[:, A_Z * 16:(A_Z + 1) * 16], atile[:, A_TT * 16:(A_TT + 1) * 16],
            idx_sb[:, c:c + 1], atile[:, A_U * 16:(A_U + 1) * 16], ALU.mult, ALU.add)))(),
            waits=[("ac", e_u), ("ac", e_tt)], inc="ve")
        e_r = E("ve", lambda h: h.reciprocal(
            atile[:, A_R * 16:(A_R + 1) * 16], atile[:, A_Z * 16:(A_Z + 1) * 16]),
            waits=[("ve", e_z)], inc="ve")
        e_A = E("ve", lambda h: h.tensor_tensor(
            atile[:, A_A * 16:(A_A + 1) * 16], atile[:, A_TT * 16:(A_TT + 1) * 16],
            atile[:, A_R * 16:(A_R + 1) * 16], ALU.mult), waits=[("ve", e_r)], inc="ve")
        e_B = E("ve", lambda h: h.tensor_tensor(
            atile[:, A_B * 16:(A_B + 1) * 16], atile[:, A_U * 16:(A_U + 1) * 16],
            atile[:, A_R * 16:(A_R + 1) * 16], ALU.mult), waits=[("ve", e_r)], inc="ve")

        if c + 1 < NCH:
            head_slow_a(c + 1)
        if c + 2 < NCH:
            head_fast_a(c + 2)

        if NCH >= 28 and not phase_a_only:
            if c >= 20 and c <= 26 and (c - 20) % 2 == 0:
                w1_mms(0, (c - 20) // 2)
            if c >= 18 and c <= 24 and (c - 18) % 2 == 0:
                w1_load_stats(0, (c - 18) // 2)

        for half in range(2):
            cols = slice(half * 512, (half + 1) * 512)
            w = [("ve" if c == 0 else "ac", carry_ve[c])]
            if c >= 1 and half == 0:
                w.append(("ve", t1_ve[c - 1]))
            E("pe", (lambda cols=cols: (lambda h: h.matmul(
                pC[:, cols], onesr[:, :], carry_b[0:1, cols], start=True, stop=False)))(),
                waits=w)
            E("pe", (lambda cols=cols, vb=vb: (lambda h: h.matmul(
                pC[:, cols], ubf[:, :], vb[:, cols], start=False, stop=True)))(),
                waits=[("ve", vbf_ve[c])] if half == 0 else [],
                inc="pe" if half == 1 else None)
        s_pe[c] = sch.cnt["pe"]
        for half in range(2):
            cols = slice(half * 512, (half + 1) * 512)
            w = []
            if half == 0:
                w = [("ve" if c == 0 else "ac", carry_ve[c])]
            E("pe", (lambda cols=cols, vb=vb: (lambda h: h.matmul(
                pD[0:1, cols], onesc[:, :], vb[:, cols],
                start=True, stop=False)))(), waits=w)
            E("pe", (lambda cols=cols: (lambda h: h.matmul(
                pD[0:1, cols], onesr[0:1, 0:1], carry_b[0:1, cols],
                start=False, stop=True)))(),
                inc="pe" if half == 1 else None)
        cs_pe[c] = sch.cnt["pe"]

        E("ac", lambda h: h.activation(
            carry_b[0:1, 0:512], pD[0:1, 0:512], AF.Copy),
            waits=[("pe", cs_pe[c])], inc="ac")
        carry_ve[c + 1] = E("ac", lambda h: h.activation(
            carry_b[0:1, 512:1024], pD[0:1, 512:1024], AF.Copy), inc="ac")

        at = attn_s[c % 2]
        tt2 = t2_s[c % 2]
        w_t1 = [("pe", s_pe[c]), ("ve", e_A)]
        if c >= 2:
            w_t1.append(("ac", apply_ac[c - 2]))
            w_t1.append(("po", t3_po[c - 2]))
        t1 = E("ve", (lambda at=at: (lambda h: h.tensor_tensor(
            at[:, :].rearrange("p (nh dh) -> p nh dh", nh=NH),
            pC[:, :].rearrange("p (nh dh) -> p nh dh", nh=NH),
            atile[:, A_A * 16:(A_A + 1) * 16].broadcast_to([128, 16, 64]),
            ALU.mult)))(), waits=w_t1, inc="ve")
        t1_ve[c] = t1
        w_t2 = [("ve", e_B)]
        if c >= 2:
            w_t2.append(("po", x1_po[c - 2]))
            w_t2.append(("ac", apply_ac[c - 2]))
        t2 = E("pl", (lambda qv=qv, tt2=tt2: (lambda h: h.tensor_tensor(
            tt2[:, :].rearrange("p (nh dh) -> p nh dh", nh=NH),
            qv[:, :].rearrange("p (nh gh) -> p nh gh", gh=GH)[:, :, 128:192],
            atile[:, A_B * 16:(A_B + 1) * 16].broadcast_to([128, 16, 64]),
            ALU.mult)))(), waits=w_t2, inc="po")
        t2_ve[c] = t2
        t3_po[c] = attn_po = E("pl", (lambda at=at, tt2=tt2: (lambda h: h.tensor_tensor(
            at[:, :], at[:, :], tt2[:, :], ALU.add)))(),
            waits=[("ve", t1), ("po", t2)], inc="po")
        attn_ve[c] = 0

        w_q2 = [("po", attn_po), ("ac", ja_last[0])]
        sq2 = E("ac", (lambda at=at: (lambda h: h.activation(
            junka[:, :], at[:, :], AF.Square,
            accum_out=stat[:, SUMSQ2:SUMSQ2 + 1])))(), waits=w_q2, inc="ac")
        ja_last[0] = sq2
        sum2 = E("ac", (lambda at=at: (lambda h: h.activation(
            junka[:, :], at[:, :], AF.Copy, accum_out=stat[:, SUM2:SUM2 + 1])))(),
            waits=[("po", attn_po), ("ac", ja_last[0])], inc="ac")
        ja_last[0] = sum2
        p_ = E("ac", lambda h: h.activation(
            stat[:, NEGM2:NEGM2 + 1], stat[:, SUM2:SUM2 + 1], AF.Copy,
            scale=cst[:, C_NINV:C_NINV + 1]), waits=[("ac", sum2)], inc="ac")
        p_ = E("ac", lambda h: h.activation(
            stat[:, TMP2:TMP2 + 1], stat[:, NEGM2:NEGM2 + 1], AF.Square),
            waits=[("ac", p_)], inc="ac")
        p_ = E("ac", lambda h: h.activation(
            stat[:, BB2:BB2 + 1], stat[:, TMP2:TMP2 + 1], AF.Identity,
            scale=cst[:, C_NEG1:C_NEG1 + 1], bias=cst[:, C_EPS:C_EPS + 1]),
            waits=[("ac", p_)], inc="ac")
        lnv2_ac[c] = p_ = E("ac", lambda h: h.activation(
            stat[:, LNV2:LNV2 + 1], stat[:, SUMSQ2:SUMSQ2 + 1], AF.Ln,
            scale=cst[:, C_INV:C_INV + 1], bias=stat[:, BB2:BB2 + 1]),
            waits=[("ac", p_)], inc="ac")
        p_ = E("ac", lambda h: h.activation(
            stat[:, RSTD2:RSTD2 + 1], stat[:, LNV2:LNV2 + 1], AF.Exp,
            scale=cst[:, C_NHALF:C_NHALF + 1]), waits=[("ac", p_)], inc="ac")
        p_ = E("ac", lambda h: h.activation(
            stat[:, NMR2:NMR2 + 1], stat[:, NEGM2:NEGM2 + 1], AF.Copy,
            scale=stat[:, RSTD2:RSTD2 + 1]), waits=[("ac", p_)], inc="ac")
        apply_ac[c] = E("ac", (lambda at=at, tt2=tt2: (lambda h: h.activation(
            tt2[:, :], at[:, :], AF.Identity,
            bias=stat[:, NMR2:NMR2 + 1], scale=stat[:, RSTD2:RSTD2 + 1])))(),
            waits=[("ac", p_)], inc="ac")
        w_x1 = [("ac", apply_ac[c])]
        if c >= 2:
            w_x1.append((f"ssw{c - 2}" if c - 2 < 4 else f"sst{c % 2}", stx1[c - 2]))
        x1_po[c] = E("pl", (lambda xt=xt, x1t=x1t, tt2=tt2: (lambda h: h.tensor_tensor(
            x1t[:, :], xt[:, :], tt2[:, :], ALU.add)))(), waits=w_x1, inc="po")
        st_tgt = out_d if phase_a_only else x1s_d
        st_sem = f"ssw{c}" if c < 4 else f"sst{c % 2}"
        stx1[c] = E("sp", (lambda x1t=x1t, rows=rows, st_tgt=st_tgt: (lambda h: h.dma_start(
            st_tgt[rows, :], x1t[:, :])))(), waits=[("po", x1_po[c])], inc=st_sem, by=16)


    arena_barrier = [(f"ssw{c}", 16) for c in range(min(4, NCH))]
    if NCH >= 6:
        arena_barrier += [("sst0", stx1[NCH - 2]), ("sst1", stx1[NCH - 1])]
    elif NCH == 5:
        arena_barrier += [("sst0", stx1[4])]
    arena_barrier += [("po", x1_po[NCH - 1]), ("ac", apply_ac[NCH - 1]),
                      ("ac", carry_ve[NCH]), ("pe", s_pe[NCH - 1])]

    E("pl", lambda h: h.nop(), waits=[("pe", cs_pe[NCH - 1]), ("po", t2_ve[NCH - 1]), ("po", ep_po[NCH - 1])])
    for j in range(8):
        E("pl", (lambda j=j: (lambda h: h.dma_start(
            w2t_sb[:, 4 * j:4 * (j + 1), :],
            w2t_d[j * 512:(j + 1) * 512, :].rearrange("(kc p) n -> p kc n", p=128))))(),
            inc="pw2", by=16)
    E("pl", lambda h: h.dma_start(b2_bf[:, :], b2r_d),
      waits=list(arena_barrier), inc="pw2", by=16)
    w2_done = b2_done = sch.cnt["pw2"]

    if not (NCH >= 28):
        for s in range(4):
            w1_load_stats(0, s)
            w1_mms(0, s)
    evod_ve = E("ve", lambda h: h.tensor_copy(
        evodfm[:, :, :], ln3fm[:, :, 1:17]),
        waits=[("ac", l3ev[0][0]), ("po", x1_po[NCH - 2]),
               ("ac", apply_ac[NCH - 2])], inc="ve")
    build["evod"] = evod_ve
    E("ve", lambda h: h.memset(r1_sb[:, :], 0.0), inc="ve")
    E("ve", lambda h: h.memset(rs_sb[:, :], 0.0),
      waits=[("po", x1_po[NCH - 2]), ("ac", apply_ac[NCH - 2])], inc="ve")
    E("ve", lambda h: h.memset(odd4ext[:, :, :], 0.0), inc="ve")
    memset_ve = sch.cnt["ve"]
    for kc in range(32):
        p_, cc = kc // 8, kc % 8
        E("ve", (lambda kc=kc, p_=p_, cc=cc: (lambda h: h.tensor_copy(
            odd4ext[:, kc, p_ * 4:(p_ + 1) * 4], evodfm[:, cc, 9:16:2])))(),
            waits=[("ve", evod_ve), ("ve", memset_ve)], inc="ve")
    build["odd4ext"] = sch.cnt["ve"]
    stag_dma = [0, 0]
    stag4_dma = [0]
    ev_slot = [0, 0]
    stags = (stag, stag2)
    ridx = 0
    for i in range(4):
        for src_col, which in ((2 * i, "r1"), (8 + 2 * i, "rs")):
            sg = stags[ridx % 2]
            prow = 32 * (ridx % 2)
            bsem = "pbld" if ridx % 2 == 0 else "pbld2"
            for kc in range(8):
                w = [("ve", evod_ve)] if kc == 0 else []
                if kc == 0:
                    if ridx < 2:
                        w.extend(pa_war[0])
                    if ev_slot[prow // 32]:
                        w.append(("ve", ev_slot[prow // 32]))
                E("pe", (lambda kc=kc, src_col=src_col, prow=prow: (lambda h: h.matmul(
                    pA[prow:prow + 1, kc * 128:(kc + 1) * 128],
                    evodfm[:, kc, src_col:src_col + 1], i128b[:, :],
                    start=True, stop=True)))(), waits=w,
                    inc="pe" if kc == 7 else None)
            row_pe = sch.cnt["pe"]
            w_ev = [("pe", row_pe)]
            if ridx == 0:
                cte = NCH - 2
                w_ev.append(("po", x1_po[cte]))
                w_ev.append((f"ssw{cte}" if cte < 4 else f"sst{cte % 2}",
                             stx1[cte]))
            if ridx == 1:
                lastx0 = ((NCH - 1) // 3) * 3
                w_ev.append(("po", x1_po[lastx0]))
                w_ev.append(("ac", xc_ac[lastx0]))
            if stag_dma[ridx % 2]:
                w_ev.append((bsem, stag_dma[ridx % 2]))
            ev = E("ve", (lambda sg=sg, prow=prow: (lambda h: h.tensor_copy(
                sg[0:1, :], pA[prow:prow + 1, 0:1024])))(), waits=w_ev, inc="ve")
            ev_slot[prow // 32] = ev
            if which == "r1":
                E("sp", (lambda i=i, sg=sg: (lambda h: h.dma_start(
                    r1_sb[i:i + 1, i * 1024:(i + 1) * 1024], sg[0:1, :])))(),
                    waits=[("ve", ev), ("ve", memset_ve)], inc=bsem, by=16)
            else:
                dw = [("ve", ev), ("ve", ess_ve[NCH - 1])]
                if stag4_dma[0]:
                    dw.append((bsem, stag4_dma[0]))
                for p_ in range(4):
                    E("ve", (lambda p_=p_, sg=sg: (lambda h: h.tensor_copy(
                        stag4[0:1, p_ * 256:(p_ + 1) * 256],
                        sg[0:1, p_:1024:4])))(),
                        waits=dw if p_ == 0 else [], inc="ve")
                deint = sch.cnt["ve"]
                for p_ in range(4):
                    E("sp", (lambda i=i, p_=p_: (lambda h: h.dma_start(
                        rs_sb[p_ * 4 + i:p_ * 4 + i + 1, i * 256:(i + 1) * 256],
                        stag4[0:1, p_ * 256:(p_ + 1) * 256])))(),
                        waits=([("ve", deint), ("ve", memset_ve)]
                               if p_ == 0 else []), inc=bsem, by=16)
                stag4_dma[0] = sch.cnt[bsem]
            stag_dma[ridx % 2] = sch.cnt[bsem]
            ridx += 1
    build["done_pwl"] = sch.cnt["pbld"]
    build["done_pwl2"] = sch.cnt["pbld2"]
    pa_war[0] = [("ve", ev_slot[0]), ("ve", ev_slot[1])]
    rs_scaled = E("ve", lambda h: h.tensor_scalar(
        rs_sb[:, :], rs_sb[:, :], scal_sb[0:16, 0:1], None, ALU.mult),
        waits=[("pbld", build["done_pwl"]), ("pbld2", build["done_pwl2"])], inc="ve")
    build["done_ve"] = rs_scaled
    tst(0)

    for sc in range(NSC):
        hrot = (pB[:, 0:512], pB[:, 512:1024], pC[:, 0:512])
        for hc in range(32):
            hp = hrot[hc % 3]
            for kc in range(8):
                w = []
                if kc == 0:
                    if hc == 0:
                        w = [("ac", l3ev[sc][3]), ("pw1", w1_done), ("ve", ts4_ve[sc])]
                        if sc == 0:
                            w.append(("ve", build["done_ve"]))
                        if sc >= 1:
                            w.append(("ve", fin_ve[sc - 1][2]))
                            w.append(("ac", sum4_ac[sc - 1][2]))
                    if hc == 2:
                        w.append(("ve", ts4_ve[sc]))
                    if hc >= 3:
                        w.append(("ac", gel_ac[sc][hc - 3]))
                E("pe", (lambda kc=kc, hc=hc, hp=hp: (lambda h: h.matmul(
                    hp, w1t_sb[:, kc, hc * 128:(hc + 1) * 128], ln3fm[:, kc, 0:512],
                    start=(kc == 0), stop=False)))(), waits=w)
            w = [("ve", ts4_ve[sc])]
            if sc == 0:
                w.append(("pbld", build["done_pwl"]))
                w.append(("pbld2", build["done_pwl2"]))
            E("pe", (lambda hc=hc, hp=hp: (lambda h: h.matmul(
                hp, r1_sb[:, hc * 128:(hc + 1) * 128], ts4[:, :],
                start=False, stop=True)))(), waits=w, inc="pe")
            h_pe[sc][hc] = sch.cnt["pe"]
            w = [("pe", h_pe[sc][hc])]
            if sc >= 1:
                w.append(("pe", o2_pe[sc - 1][3]))
            gel_ac[sc][hc] = E("ac", (lambda hc=hc, hp=hp: (lambda h: h.activation(
                h_fm[:, hc, :], hp, AF.Gelu, bias=b1c_sb[:, hc:hc + 1])))(),
                waits=w, inc="ac")
            if hc == 19 and sc + 1 < NSC:
                w1_load_stats(sc + 1, 0)
        if sc + 1 < NSC:
            w1_mms(sc + 1, 0)
            w1_load_stats(sc + 1, 1)

        for kc in range(32):
            w = []
            if kc == 0:
                w = [("ac", gel_ac[sc][31]), ("ve", ts4_ve[sc])]
                if sc == 0:
                    w.append(("ve", build["odd4ext"]))
                    w.append(("ac", carry_ve[NCH]))
                else:
                    w.append(("ve", fin_ve[sc - 1][3]))
                    w.append(("ve", gtbf_ve[sc - 1]))
            E("pe", (lambda kc=kc: (lambda h: h.matmul(
                pD[0:16, 0:512], odd4ext[:, kc, :], h_fm[:, kc, 0:512],
                start=(kc == 0), stop=(kc == 31))))(), waits=w,
                inc="pe" if kc == 31 else None)
        gt_pe[sc] = sch.cnt["pe"]
        gtbf_ve[sc] = E("ve", lambda h: h.tensor_copy(gt_bf[:, :], pD[0:16, 0:512]),
                        waits=[("pe", gt_pe[sc])], inc="ve")
        if sc + 1 < NSC:
            w1_mms(sc + 1, 1)
            w1_load_stats(sc + 1, 2)

        for s in range(4):
            cglob = sc * 4 + s
            slot2 = x1i[s % 2]
            w = []
            if s < 2:
                if sc >= 1:
                    w.append((f"so{s % 2}", stout[sc - 1][s + 2]))
                else:
                    w.extend(arena_barrier)
            else:
                w.append((f"so{s % 2}", stout[sc][s - 2]))
            ldb[sc][s] = E("sp", (lambda slot=slot2, cglob=cglob: (lambda h: h.dma_start(
                slot[:, :], x1s_d[cglob * 128:(cglob + 1) * 128, :])))(),
                waits=w, inc=f"sb{s % 2}", by=16)
            pt = pB if s % 2 == 0 else pD
            for kc in range(32):
                for half in range(2):
                    cols = slice(half * 512, (half + 1) * 512)
                    w = []
                    if kc == 0 and half == 0:
                        w = [("pw2", w2_done), ("ve", gtbf_ve[sc])]
                        if s % 2 == 0:
                            w.append(("ac", gel_ac[sc][31]))
                        if s >= 2:
                            w.append(("ve", fin_ve[sc][s - 2]))
                            w.append(("ac", sum4_ac[sc][s - 2]))
                    E("pe", (lambda kc=kc, s=s, pt=pt, cols=cols: (lambda h: h.matmul(
                        pt[:, cols], h_fm[:, kc, s * 128:(s + 1) * 128],
                        w2t_sb[:, kc, cols], start=(kc == 0), stop=False)))(), waits=w)
            for half in range(2):
                cols = slice(half * 512, (half + 1) * 512)
                w = []
                if half == 0 and sc == 0:
                    w = [("ve", build["done_ve"])]
                E("pe", (lambda s=s, pt=pt, cols=cols: (lambda h: h.matmul(
                    pt[:, cols], gt_bf[:, s * 128:(s + 1) * 128], rs_sb[0:16, cols],
                    start=False, stop=False)))(), waits=w)
                E("pe", (lambda pt=pt, cols=cols: (lambda h: h.matmul(
                    pt[:, cols], onesr[:, :], b2_bf[0:1, cols],
                    start=False, stop=True)))(),
                    waits=[("pw2", b2_done)] if half == 0 else [],
                    inc="pe" if half == 1 else None)
            o2_pe[sc][s] = sch.cnt["pe"]

            if sc + 1 < NSC:
                if s == 0:
                    w1_mms(sc + 1, 2)
                    w1_load_stats(sc + 1, 3)
                elif s == 1:
                    w1_mms(sc + 1, 3)
                elif s == 2:
                    tst(sc + 1)

            w_sq4 = [("pe", o2_pe[sc][s])]
            prev_nm4w = (nmr4_ac[sc][s - 1] if s > 0 else
                         (nmr4_ac[sc - 1][3] if sc > 0 else apply_ac[NCH - 1]))
            w_sq4.append(("ac", prev_nm4w))
            w_sq4.append(("ve", jv_last[0]))
            sum4 = E("ve", (lambda pt=pt: (lambda h: h.tensor_scalar(
                junkv[:, :], pt[:, :], 1.0, 0.0, ALU.mult, ALU.add,
                accum_out=stat[:, SUM2:SUM2 + 1])))(), waits=w_sq4, inc="ve")
            jv_last[0] = sum4
            sq4 = E("ac", lambda h: h.activation(
                junka[:, :], junkv[:, :], AF.Square,
                accum_out=stat[:, SUMSQ2:SUMSQ2 + 1]),
                waits=[("ve", sum4), ("ac", ja_last[0])], inc="ac")
            ja_last[0] = sq4
            sum4_ac[sc][s] = sq4
            p_ = E("ac", lambda h: h.activation(
                stat[:, NEGM2:NEGM2 + 1], stat[:, SUM2:SUM2 + 1], AF.Copy,
                scale=cst[:, C_NINV:C_NINV + 1]),
                waits=[("ve", sum4), ("ac", sq4)], inc="ac")
            p_ = E("ac", lambda h: h.activation(
                stat[:, TMP2:TMP2 + 1], stat[:, NEGM2:NEGM2 + 1], AF.Square),
                waits=[("ac", p_)], inc="ac")
            p_ = E("ac", lambda h: h.activation(
                stat[:, BB2:BB2 + 1], stat[:, TMP2:TMP2 + 1], AF.Identity,
                scale=cst[:, C_NEG1:C_NEG1 + 1], bias=cst[:, C_EPS:C_EPS + 1]),
                waits=[("ac", p_)], inc="ac")
            lnv4 = p_ = E("ac", lambda h: h.activation(
                stat[:, LNV2:LNV2 + 1], stat[:, SUMSQ2:SUMSQ2 + 1], AF.Ln,
                scale=cst[:, C_INV:C_INV + 1], bias=stat[:, BB2:BB2 + 1]),
                waits=[("ac", p_)], inc="ac")
            lnv4_ac[sc][s] = lnv4
            p_ = E("ac", lambda h: h.activation(
                stat[:, RSTD2:RSTD2 + 1], stat[:, LNV2:LNV2 + 1], AF.Exp,
                scale=cst[:, C_NHALF:C_NHALF + 1]), waits=[("ac", p_)], inc="ac")
            nmr4_ac[sc][s] = E("ac", lambda h: h.activation(
                stat[:, NMR2:NMR2 + 1], stat[:, NEGM2:NEGM2 + 1], AF.Copy,
                scale=stat[:, RSTD2:RSTD2 + 1]), waits=[("ac", p_)], inc="ac")
            xs = E("ve", (lambda slot=slot2: (lambda h: h.tensor_scalar(
                slot[:, :], slot[:, :], stat[:, NMR2:NMR2 + 1], None, ALU.add)))(),
                waits=[("ac", nmr4_ac[sc][s]), (f"sb{s % 2}", ldb[sc][s])], inc="ve")
            fin_ve[sc][s] = E("ve", (lambda pt=pt, slot=slot2: (lambda h: h.scalar_tensor_tensor(
                slot[:, :], pt[:, :], stat[:, RSTD2:RSTD2 + 1], slot[:, :],
                ALU.mult, ALU.add)))(), waits=[("ve", xs)], inc="ve")
            stout[sc][s] = E("sp", (lambda sc=sc, s=s, slot=slot2: (lambda h: h.dma_start(
                out_d[(sc * 4 + s) * 128:(sc * 4 + s + 1) * 128, :], slot[:, :])))(),
                waits=[("ve", fin_ve[sc][s])], inc=f"so{s % 2}", by=16)

    E("sp", lambda h: h.nop(),
      waits=[("so0", sch.cnt["so0"]), ("so1", sch.cnt["so1"])])
    return _finish(nc, sch)


def _finish(nc, sch):
    from contextlib import ExitStack
    es = ExitStack()
    sems = {s: es.enter_context(nc.semaphore(f"{s}_sem")) for s in Sched.SEMS}
    block = es.enter_context(nc.Block())

    @block.tensor
    def _(h):
        sch.replay("pe", h, sems)

    @block.vector
    def _(h):
        sch.replay("ve", h, sems)

    @block.scalar
    def _(h):
        sch.replay("ac", h, sems)

    @block.sync
    def _(h):
        sch.replay("sp", h, sems)

    @block.gpsimd
    def _(h):
        sch.replay("pl", h, sems)

    es.close()
    return nc


def _host_arrays(S, NCH):
    bfloat = ml_dtypes.bfloat16
    cst = np.zeros((128, 8), dtype=np.float32)
    cst[:, C_EPS] = EPS
    cst[:, C_NINV] = -1.0 / D
    cst[:, C_INV] = 1.0 / D
    cst[:, C_NHALF] = -0.5
    cst[:, C_NEG1] = -1.0
    idx = (np.arange(NCH)[None, :] * 128 + np.arange(128)[:, None]).astype(np.float32)
    ubf = np.triu(np.ones((128, 128), dtype=np.float32), k=1).astype(bfloat)
    i128f = np.eye(128, dtype=np.float32)
    i128b = np.eye(128, dtype=np.float32).astype(bfloat)
    onesr = np.ones((1, 128), dtype=np.float32).astype(bfloat)
    onesc = np.ones((128, 1), dtype=np.float32).astype(bfloat)
    return cst, idx, ubf, i128f, i128b, onesr, onesc


def kernel(L0_attn_x, L0_attn_wqkv, L0_ffn_W1, L0_ffn_b1, L0_ffn_W2, L0_ffn_b2,
           L0_ffn_scale):
    x = np.ascontiguousarray(np.asarray(L0_attn_x, dtype=np.float32))
    B, S, _ = x.shape
    NCH = S // 128
    wqkv = np.ascontiguousarray(np.asarray(L0_attn_wqkv, dtype=np.float32))
    w1t = np.ascontiguousarray(np.asarray(L0_ffn_W1, dtype=np.float32).T)
    w2t = np.ascontiguousarray(np.asarray(L0_ffn_W2, dtype=np.float32).T)
    b1c = np.ascontiguousarray(
        np.asarray(L0_ffn_b1, dtype=np.float32).reshape(H // 128, 128).T)
    b2r = np.ascontiguousarray(np.asarray(L0_ffn_b2, dtype=np.float32).reshape(1, D))
    scal = np.full((128, 1), np.float32(np.asarray(L0_ffn_scale)), dtype=np.float32)
    cst, idx, ubf, i128f, i128b, onesr, onesc = _host_arrays(S, NCH)

    nc = build_nc(S)
    shared = dict(wqkv=wqkv, w1t=w1t, w2t=w2t, b1c=b1c, b2r=b2r, cst=cst, idx=idx,
                  scal=scal, ubf=ubf, i128f=i128f, i128b=i128b, onesr=onesr,
                  onesc=onesc)
    in_maps = [dict(x=np.ascontiguousarray(x[b]), **shared) for b in range(B)]
    res = run_bass_kernel_spmd(nc, in_maps, core_ids=list(range(B)))
    return np.stack([r["out"] for r in res.results], axis=0)
